# version 08
# baseline (speedup 1.0000x reference)
import numpy as np
from contextlib import ExitStack

import concourse.bass as bass
import concourse.mybir as mybir
from concourse.bass_utils import run_bass_kernel_spmd

F32 = mybir.dt.float32
BF16 = mybir.dt.bfloat16
AF = mybir.ActivationFunctionType
ALU = mybir.AluOpType

NCORE = 8
D = 4096
NT = 32
SL = 1024
NH2 = 512
PROJ_PAD = 8576
MT_IP = 67
CONVT = 34
XBCW = 1028
Q = 128
NCH = 8
NHD = 64
HD = 64
EPS = 1e-5

PHASES = ["p0", "ln1", "p1", "p2a", "p2b", "p2c", "p2d", "p2e", "p2f", "p2h", "p3"]


class Sem:
    def __init__(self, stack, nc, name):
        self.h = stack.enter_context(nc.semaphore(name))
        self.n = 0
        self.name = name

    def inc(self, instr, k=1):
        instr.then_inc(self.h, k)
        self.n += k
        return (self, self.n)

    def tok(self):
        return (self, self.n)


def wait(eng, *tokens):
    for t in tokens:
        if t is None:
            continue
        sem, val = t
        if val > 0:
            eng.wait_ge(sem.h, val)


def build_program(limit="p3", dbg=False, p1_stats=True, p1_sgn=True, p2b_cc=True):
    nc = bass.Bass("TRN2", target_bir_lowering=False)
    limit_i = PHASES.index(limit)

    def on(ph):
        return PHASES.index(ph) <= limit_i

    from concourse.hw_specs import get_activation_tables
    _tab_ids = {}
    for _i, (_nm, _funcs) in enumerate(get_activation_tables(nc.m.arch).items()):
        _tab_ids[_nm] = _i

    _all_sems = []

    def bar_toks():
        return [sem.tok() for sem in _all_sems]

    def load_act(s_eng, set_name):
        s_eng.add_instruction(mybir.InstLoadActFuncSet(
            name=nc.get_next_instruction_name(),
            act_func_set_id=_tab_ids[set_name], ins=[], outs=[]))

    ein = lambda n, s, d_: nc.dram_tensor(n, s, d_, kind="ExternalInput").ap()
    xin = ein("xin", [D, SL], F32)
    wffn = ein("wffn", [D, D], F32)
    wip = ein("wip", [D, PROJ_PAD], F32)
    wop = ein("wop", [D, D], F32)
    lnw_c = ein("lnw_c", [128, NT], F32)
    lnw_rd = ein("lnw_r", [1, D], F32)
    nrmw_c = ein("nrmw_c", [128, NT], F32)
    bias_c = ein("bias_c", [128, NT], F32)
    mup_c = ein("mup_c", [128, MT_IP], F32)
    cw_c = ein("cw_c", [128, CONVT * 4], F32)
    drep_d = ein("drep", [128, NHD], F32)
    dtb_d = ein("dtb", [128, 1], F32)
    alog_d = ein("alog", [128, 1], F32)
    or32_d = ein("or32", [1, 128], F32)
    id32_d = ein("id32", [128, 128], F32)
    mask01_d = ein("mask01", [128, 128], F32)
    oc16_d = ein("oc16", [128, 1], BF16)
    id16_d = ein("id16", [128, 128], BF16)
    ones32_d = ein("ones32", [128, 128], F32)
    hsel_d = ein("hsel", [128, 8], F32)
    pfx8_d = ein("pfx8", [8, NHD], F32)
    selr2_d = ein("selr2", [8, 8], F32)

    out_d = nc.dram_tensor("out", [D, SL], F32, kind="ExternalOutput").ap()

    ikind = "ExternalOutput" if dbg else "Internal"
    XR = nc.dram_tensor("XR", [D, SL], F32, kind=ikind).ap()
    SGN = nc.dram_tensor("SGN", [D, D], BF16, kind="Internal").ap()
    ZSP = nc.dram_tensor("ZSP", [D, SL], BF16, kind=ikind).ap()
    cc1i = nc.dram_tensor("cc1i", [128, CONVT * 3], BF16)
    cc1o = nc.dram_tensor("cc1o", [128 * NCORE, CONVT * 3], BF16)
    cc2hi = nc.dram_tensor("cc2hi", [128, D], BF16)
    cc2ho = nc.dram_tensor("cc2ho", [128 * NCORE, D], BF16)
    cc2li = nc.dram_tensor("cc2li", [1, NHD], F32)
    cc2lo = nc.dram_tensor("cc2lo", [NCORE, NHD], F32)
    dbgt = {}
    if dbg:
        for nm, shp in [("DXC", [D, SL]), ("DXB", [128, CONVT * XBCW]),
                        ("DXS", [128, NCH * D]), ("DBT", [128, SL]), ("DCT", [128, SL]),
                        ("DDT", [128, SL]), ("DLA", [128, SL]), ("DHZ", [128, D]),
                        ("DHI", [128, D]), ("DG", [128, NT * SL]), ("DYN", [128, NT * SL]),
                        ("DX1", [D, SL])]:
            dbgt[nm] = nc.dram_tensor(nm, shp, F32, kind="ExternalOutput").ap()

    st = ExitStack()
    with st:
        dW = Sem(st, nc, "dW")
        dG = Sem(st, nc, "dG")
        dS = Sem(st, nc, "dS")
        pe = Sem(st, nc, "pe")
        ve = Sem(st, nc, "ve")
        se = Sem(st, nc, "se")
        ccs = Sem(st, nc, "ccs")
        fin = Sem(st, nc, "fin")
        _all_sems.extend([dW, dG, dS, pe, ve, se, ccs])
        vch = Sem(st, nc, "vch")
        sch = Sem(st, nc, "sch")
        gch = Sem(st, nc, "gch")
        dCst = Sem(st, nc, "dCst")
        dX = [Sem(st, nc, "dX0"), Sem(st, nc, "dX1")]
        dWs = [Sem(st, nc, "dWs0"), Sem(st, nc, "dWs1")]
        dR = [Sem(st, nc, "dR0"), Sem(st, nc, "dR1")]
        dSt = [Sem(st, nc, "dSt0"), Sem(st, nc, "dSt1")]
        dN = [Sem(st, nc, "dN0"), Sem(st, nc, "dN1")]
        dZ = [Sem(st, nc, "dZ0"), Sem(st, nc, "dZ1")]
        dCC = Sem(st, nc, "dCC")
        dWg = [Sem(st, nc, "dWg0"), Sem(st, nc, "dWg1")]
        dRH = [Sem(st, nc, "dRH0"), Sem(st, nc, "dRH1")]
        _all_sems.extend([dCst, dX[0], dX[1], dWs[0], dWs[1], dR[0], dR[1],
                          dSt[0], dSt[1], dN[0], dN[1], dZ[0], dZ[1], dCC,
                          dWg[0], dWg[1], dRH[0], dRH[1]])

        def VC(v_eng, ins):
            wait(v_eng, vch.inc(ins))
            return ins

        def SC(s_eng, ins):
            wait(s_eng, sch.inc(ins))
            return ins

        def inc_chain(eng, sem, ins):
            tok = sem.inc(ins)
            wait(eng, tok)
            return tok

        SB = lambda n, s, d_, side=None: st.enter_context(
            nc.sbuf_tensor(n, s, d_) if side is None else nc.sbuf_tensor(n, s, d_, side=side))

        c_lnw = SB("c_lnw", [128, NT], F32)
        c_nrmw = SB("c_nrmw", [128, NT], F32)
        c_hb = SB("c_hb", [128, NT], F32)
        c_mup = SB("c_mup", [128, MT_IP], F32)
        c_cw = SB("c_cw", [128, CONVT * 4], F32)
        c_drep = SB("c_drep", [128, NHD], F32)
        c_dtb = SB("c_dtb", [128, 1], F32)
        c_alog = SB("c_alog", [128, 1], F32)
        c_or32 = SB("c_or32", [1, 128], F32)
        c_id32 = SB("c_id32", [128, 128], F32)
        c_mask01 = SB("c_mask01", [128, 128], F32)
        c_oc16 = SB("c_oc16", [128, 1], BF16)
        c_id16 = SB("c_id16", [128, 128], BF16)
        c_ones32 = SB("c_ones32", [128, 128], F32)
        c_hsel = SB("c_hsel", [128, 8], F32)
        c_pfx8 = SB("c_pfx8", [8, NHD], F32)
        c_selr2 = SB("c_selr2", [8, 8], F32)
        c_acol = SB("c_acol", [128, 1], F32)
        c_eps1 = SB("c_eps1", [1, 1], F32)
        c_one1 = SB("c_one1", [128, 1], F32)
        r_m = SB("r_m", [1, SL], F32)
        r_scl = SB("r_scl", [1, SL], F32)
        r_mscl = SB("r_mscl", [1, SL], F32)
        r_t1 = SB("r_t1", [1, SL], F32)
        r_t2 = SB("r_t2", [1, SL], F32)
        BIGA = SB("BIGA", [128, NT * SL], BF16, side="right")
        BIGB = SB("BIGB", [128, CONVT * XBCW], BF16, side="right")

        biga_t = lambda k: BIGA[:, k * SL:(k + 1) * SL]
        bigb_t = lambda k: BIGB[:, k * SL:(k + 1) * SL]

        def strip_view(w, m):
            return w.rearrange("(k p) mj -> p k mj", p=128)[:, :, m * 128:(m + 1) * 128]

        def sgn_strip_view(m):
            return SGN[m * 128:(m + 1) * 128, :].rearrange("p (k j) -> p k j", k=NT)

        with nc.Block() as blk:
            @blk.gpsimd
            def _(g):
                for dst, src in [(c_lnw, lnw_c), (c_nrmw, nrmw_c), (c_hb, bias_c),
                                 (c_mup, mup_c), (c_cw, cw_c), (c_drep, drep_d),
                                 (c_dtb, dtb_d), (c_alog, alog_d), (c_or32, or32_d),
                                 (c_id32, id32_d), (c_mask01, mask01_d), (c_oc16, oc16_d),
                                 (c_id16, id16_d), (c_ones32, ones32_d), (c_hsel, hsel_d),
                                 (c_pfx8, pfx8_d), (c_selr2, selr2_d)]:
                    dCst.inc(g.dma_start(out=dst[:, :], in_=src), 16)

            @blk.vector
            def _(v):
                v.memset(c_eps1[:, :], EPS)
                ve.inc(v.memset(c_one1[:, :], 1.0))

            @blk.scalar
            def _(s):
                load_act(s, "natural_log_exp_and_others")
                wait(s, dCst.tok(), (ve, ve.n))
                s.mul(c_hb[:, :], c_hb[:, :], 0.5)
                SC(s, s.activation(c_acol[:, :], c_alog[:, :], AF.Exp))
                se.inc(s.mul(c_acol[:, :], c_acol[:, :], -1.0))
        setup_tok = se.tok()

        def emit_ln_rows(blk, fold_rms, entry_toks):
            v0, s0 = ve.n, se.n

            @blk.scalar
            def _(s):
                wait(s, (ve, v0 + 2))
                SC(s, s.activation(r_t1[0:1, :], r_t2[0:1, :], AF.Ln))
                se.inc(s.activation(r_scl[0:1, :], r_t1[0:1, :], AF.Exp, scale=-0.5))

            @blk.vector
            def _(v):
                wait(v, *entry_toks)
                VC(v, v.tensor_tensor(r_t2[0:1, :], r_m[0:1, :], r_m[0:1, :], ALU.mult))
                inc_chain(v, ve, v.tensor_tensor(r_t1[0:1, :], r_t1[0:1, :], r_t2[0:1, :], ALU.subtract))
                if not fold_rms:
                    ve.inc(v.tensor_scalar(r_t2[0:1, :], r_t1[0:1, :], float(EPS), None, ALU.add))
                else:
                    VC(v, v.tensor_scalar(r_scl[0:1, :], r_t1[0:1, :], float(EPS), None, ALU.add))
                    VC(v, v.reciprocal(r_t2[0:1, :], r_scl[0:1, :]))
                    VC(v, v.tensor_tensor(r_t2[0:1, :], r_t1[0:1, :], r_t2[0:1, :], ALU.mult))
                    VC(v, v.tensor_scalar(r_t2[0:1, :], r_t2[0:1, :], float(EPS), None, ALU.add))
                    ve.inc(v.tensor_tensor(r_t2[0:1, :], r_t2[0:1, :], r_scl[0:1, :], ALU.mult))
                wait(v, (se, s0 + 1))
                ve.inc(v.tensor_tensor(r_mscl[0:1, :], r_m[0:1, :], r_scl[0:1, :], ALU.mult))
            return ve.tok()

        def emit_bcast_rows(blk, entry_tok, P_x, P_y, row_x, row_y, extra=()):
            @blk.tensor
            def _(t):
                wait(t, entry_tok, *extra)
                t.matmul(P_x[:, 0:NH2], c_or32[0:1, :], row_x[0:1, 0:NH2], start=True, stop=True)
                t.matmul(P_x[:, NH2:SL], c_or32[0:1, :], row_x[0:1, NH2:SL], start=True, stop=True)
                t.matmul(P_y[:, 0:NH2], c_or32[0:1, :], row_y[0:1, 0:NH2], start=True, stop=True)
                pe.inc(t.matmul(P_y[:, NH2:SL], c_or32[0:1, :], row_y[0:1, NH2:SL], start=True, stop=True))
            return pe.tok()

        ctx0 = ExitStack()
        sq0 = ctx0.enter_context(nc.sbuf_tensor("sq0", [128, 2 * SL], BF16))
        psc0 = ExitStack()
        P_sa = psc0.enter_context(nc.psum_tensor("P_sa", [1, NH2], F32))
        P_sb = psc0.enter_context(nc.psum_tensor("P_sb", [1, NH2], F32))
        P_sc = psc0.enter_context(nc.psum_tensor("P_sc", [1, NH2], F32))
        P_sd = psc0.enter_context(nc.psum_tensor("P_sd", [1, NH2], F32))

        se0, pe0 = se.n, pe.n
        xld_tok = {}
        with nc.Block() as blk:
            @blk.gpsimd
            def _(g):
                for k in range(NT):
                    if k >= 2:
                        wait(g, (se, se0 + (k - 1)))
                    xld_tok[k] = dX[k % 2].inc(
                        g.dma_start(out=biga_t(k), in_=xin[k * 128:(k + 1) * 128, :]), 16)

            @blk.scalar
            def _(s):
                wait(s, setup_tok)
                for k in range(NT):
                    wait(s, xld_tok[k])
                    if k >= 2:
                        wait(s, (pe, pe0 + (k - 1)))
                    sl = (k % 2) * SL
                    se.inc(s.activation(sq0[:, sl:sl + SL], biga_t(k), AF.Square))

            @blk.tensor
            def _(t):
                for k in range(NT):
                    wait(t, xld_tok[k], (se, se0 + (k + 1)))
                    first, last = (k == 0), (k == NT - 1)
                    sl = (k % 2) * SL
                    t.matmul(P_sa[0:1, :], c_oc16[:, :], biga_t(k)[:, 0:NH2], start=first, stop=last)
                    t.matmul(P_sb[0:1, :], c_oc16[:, :], biga_t(k)[:, NH2:SL], start=first, stop=last)
                    t.matmul(P_sc[0:1, :], c_oc16[:, :], sq0[:, sl:sl + NH2], start=first, stop=last)
                    pe.inc(t.matmul(P_sd[0:1, :], c_oc16[:, :], sq0[:, sl + NH2:sl + SL], start=first, stop=last))

            @blk.vector
            def _(v):
                wait(v, (pe, pe0 + NT))
                v.tensor_scalar(r_m[0:1, 0:NH2], P_sa[0:1, :], 1.0 / D, None, ALU.mult)
                v.tensor_scalar(r_m[0:1, NH2:SL], P_sb[0:1, :], 1.0 / D, None, ALU.mult)
                v.tensor_scalar(r_t1[0:1, 0:NH2], P_sc[0:1, :], 1.0 / D, None, ALU.mult)
                ve.inc(v.tensor_scalar(r_t1[0:1, NH2:SL], P_sd[0:1, :], 1.0 / D, None, ALU.mult))
        x_loaded = [xld_tok[NT - 2], xld_tok[NT - 1]]

        with nc.Block() as blk:
            rows1_tok = emit_ln_rows(blk, False, [ve.tok()])
        psc0.close()
        if dbg:
            with nc.Block() as blk:
                @blk.gpsimd
                def _(g):
                    wait(g, rows1_tok)
                    dG.inc(g.dma_start(out=dbgt["DXC"], in_=BIGA[:, :]), 16)
                    dG.inc(g.dma_start(out=dbgt["DDT"][0:1, :], in_=r_m[0:1, :]), 16)
                    dG.inc(g.dma_start(out=dbgt["DDT"][1:2, :], in_=r_t1[0:1, :]), 16)
                    dG.inc(g.dma_start(out=dbgt["DDT"][2:3, :], in_=r_scl[0:1, :]), 16)
                    dG.inc(g.dma_start(out=dbgt["DDT"][3:4, :], in_=r_mscl[0:1, :]), 16)
                    dG.inc(g.dma_start(out=dbgt["DXB"][:, 0:2 * SL], in_=sq0[:, :]), 16)

        ln1_tok = None
        pscB = ExitStack()
        P_ba = pscB.enter_context(nc.psum_tensor("P_ba", [128, SL], F32))
        P_bb = pscB.enter_context(nc.psum_tensor("P_bb", [128, SL], F32))
        tmpf0 = ctx0.enter_context(nc.sbuf_tensor("tmpf0", [128, SL], F32))
        SCLB = ctx0.enter_context(nc.sbuf_tensor("sclb", [128, SL], F32))
        MSCB = ctx0.enter_context(nc.sbuf_tensor("mscb", [128, SL], F32))
        if on("ln1"):
            with nc.Block() as blk:
                bc_tok = emit_bcast_rows(blk, rows1_tok, P_ba, P_bb, r_scl, r_mscl)

                @blk.vector
                def _(v):
                    wait(v, bc_tok, *x_loaded)
                    v.tensor_copy(SCLB[:, :], P_ba[:, :])
                    VC(v, v.tensor_copy(MSCB[:, :], P_bb[:, :]))
                    for k in range(NT):
                        VC(v, v.tensor_tensor(tmpf0[:, :], biga_t(k), SCLB[:, :], ALU.mult))
                        inc_chain(v, ve, v.tensor_tensor(bigb_t(k), tmpf0[:, :], MSCB[:, :], ALU.subtract))
            ln1_tok = ve.tok()
        pscB.close()
        ctx0.close()

        _ffn_seq = [0]

        def ffn_pass(blk, opnd_get, opnd_tok, w_src, w_mode, resid_src, dst_dram,
                     do_stats, half, use_bias):
            _ffn_seq[0] += 1
            _sfx = str(_ffn_seq[0])
            """w_mode: 'sign' (fp32 half-strips -> Sign -> bf16, store to SGN),
                       'sgn16' (bf16 strips from SGN via sync),
                       'cast' (fp32 -> bf16 DMA-cast via gpsimd).
            Evac: X1 = half*psum + resid (+ c_hb col). Stores X1 to dst_dram via
            scalar queue. Optionally accumulates mean/E[x^2] into r_m/r_t1."""
            ctx = ExitStack()
            psx = ExitStack()
            HD2 = D // 2
            if w_mode == "sign":
                WF = ctx.enter_context(nc.sbuf_tensor("wf" + _sfx, [128, HD2], F32))
            WST = ctx.enter_context(nc.sbuf_tensor("wst" + _sfx, [128, 2 * D], BF16))
            XT = ctx.enter_context(nc.sbuf_tensor("xt" + _sfx, [128, 2 * SL], F32))
            X1 = ctx.enter_context(nc.sbuf_tensor("x1" + _sfx, [128, 2 * SL], F32))
            if do_stats:
                XB = ctx.enter_context(nc.sbuf_tensor("xb" + _sfx, [128, SL], BF16))
                SQ = ctx.enter_context(nc.sbuf_tensor("sq" + _sfx, [128, SL], BF16))
            P_ma = psx.enter_context(nc.psum_tensor("pma" + _sfx, [128, SL], F32))
            P_mb = psx.enter_context(nc.psum_tensor("pmb" + _sfx, [128, SL], F32))
            if do_stats:
                Psa = psx.enter_context(nc.psum_tensor("psa" + _sfx, [1, NH2], F32))
                Psb = psx.enter_context(nc.psum_tensor("psb" + _sfx, [1, NH2], F32))
                Psc = psx.enter_context(nc.psum_tensor("psc" + _sfx, [1, NH2], F32))
                Psd = psx.enter_context(nc.psum_tensor("psd" + _sfx, [1, NH2], F32))

            pe0, ve0, se0 = pe.n, ve.n, se.n
            BARF = bar_toks()
            SIGN = (w_mode == "sign")
            wdma_tok = {}
            xt_dma_tok = {}
            st_tok = {}
            sg_tok = {}

            VPM = 1 + (1 if use_bias else 0) + (1 if do_stats else 0)

            def v1_tok(m):
                return (ve, ve0 + VPM * m + 1)

            def vlast_tok(m):
                return (ve, ve0 + VPM * (m + 1))

            def pe_mm(m):
                if not do_stats:
                    return (pe, pe0 + m + 1)
                return (pe, pe0 + (1 if m == 0 else 2 * m))

            def pe_st(j):
                return (pe, pe0 + (2 * NT if j == NT - 1 else 2 * j + 3))

            SGI = 2 if SIGN else 0

            def sign_b_tok(m):
                if do_stats:
                    return (se, se0 + (2 if m == 0 else (SGI + 1) * m + SGI - 1))
                return (se, se0 + SGI * (m + 1))

            def sign_a_tok(m):
                sem, v = sign_b_tok(m)
                return (sem, v - 1)

            def sq_tok(j):
                if SIGN:
                    return (se, se0 + ((SGI + 1) * j + SGI + 3 if j < NT - 1
                                       else (SGI + 1) * (NT - 1) + SGI + 1))
                return (se, se0 + j + 1)

            def x1store_tok(j):
                return st_tok[j]

            if w_mode in ("sign", "sgn16"):
                @blk.sync
                def _(sy):
                    wait(sy, *BARF)
                    for m in range(NT):
                        if SIGN:
                            if m >= 1:
                                wait(sy, sign_b_tok(m - 1))
                            wdma_tok[(m, 0)] = dWs[0].inc(sy.dma_start(
                                out=WF[:, :].rearrange("p (k j) -> p k j", k=NT // 2),
                                in_=w_src(m)[:, 0:NT // 2, :]), 16)
                            wait(sy, sign_a_tok(m))
                            wdma_tok[(m, 1)] = dWs[1].inc(sy.dma_start(
                                out=WF[:, :].rearrange("p (k j) -> p k j", k=NT // 2),
                                in_=w_src(m)[:, NT // 2:NT, :]), 16)
                        else:
                            if m >= 2:
                                wait(sy, pe_mm(m - 2))
                            wdma_tok[m] = dWs[m % 2].inc(sy.dma_start(
                                out=WST[:, (m % 2) * D:(m % 2 + 1) * D].rearrange(
                                    "p (k j) -> p k j", k=NT),
                                in_=w_src(m)), 16)

            @blk.gpsimd
            def _(g):
                wait(g, *BARF)
                for m in range(NT):
                    if w_mode == "cast":
                        if m >= 2:
                            wait(g, pe_mm(m - 2))
                        wdma_tok[m] = dWg[m % 2].inc(g.dma_start(
                            out=WST[:, (m % 2) * D:(m % 2 + 1) * D].rearrange(
                                "p (k j) -> p k j", k=NT),
                            in_=w_src(m)), 16)
                    if m >= 2:
                        wait(g, v1_tok(m - 2))
                    xt_dma_tok[m] = dR[m % 2].inc(
                        g.dma_start(out=XT[:, (m % 2) * SL:(m % 2 + 1) * SL],
                                    in_=resid_src[m * 128:(m + 1) * 128, :]), 16)

            def wcast_tok(m):
                return wdma_tok[m]

            def xt_tok(m):
                return xt_dma_tok[m]

            @blk.scalar
            def _(s):
                wait(s, *BARF)
                for m in range(NT):
                    if SIGN:
                        wsl = (m % 2) * D
                        wait(s, wdma_tok[(m, 0)])
                        if m >= 2:
                            wait(s, pe_mm(m - 2), sg_tok[m - 2])
                        inc_chain(s, se, s.activation(WST[:, wsl:wsl + HD2], WF[:, :], AF.Sign))
                        wait(s, wdma_tok[(m, 1)])
                        inc_chain(s, se, s.activation(WST[:, wsl + HD2:wsl + D], WF[:, :], AF.Sign))
                        sg_tok[m] = dN[m % 2].inc(
                            s.dma_start(out=SGN[m * 128:(m + 1) * 128, :],
                                        in_=WST[:, wsl:wsl + D]), 16)
                    if m >= 1:
                        j = m - 1
                        wait(s, vlast_tok(j))
                        if do_stats:
                            if j >= 1:
                                wait(s, pe_st(j - 1))
                            se.inc(s.activation(SQ[:, :], X1[:, (j % 2) * SL:(j % 2 + 1) * SL],
                                                AF.Square))
                        st_tok[j] = dSt[j % 2].inc(
                            s.dma_start(out=dst_dram[j * 128:(j + 1) * 128, :],
                                        in_=X1[:, (j % 2) * SL:(j % 2 + 1) * SL]), 16)
                j = NT - 1
                wait(s, vlast_tok(j))
                if do_stats:
                    wait(s, pe_st(j - 1))
                    se.inc(s.activation(SQ[:, :], X1[:, (j % 2) * SL:(j % 2 + 1) * SL], AF.Square))
                st_tok[j] = dSt[j % 2].inc(
                    s.dma_start(out=dst_dram[j * 128:(j + 1) * 128, :],
                                in_=X1[:, (j % 2) * SL:(j % 2 + 1) * SL]), 16)

            @blk.tensor
            def _(t):
                wait(t, *BARF, opnd_tok)
                for m in range(NT):
                    if SIGN:
                        wait(t, sign_b_tok(m))
                    else:
                        wait(t, wcast_tok(m))
                    if m >= 2:
                        wait(t, v1_tok(m - 2))
                    ps = P_ma if m % 2 == 0 else P_mb
                    wsl = (m % 2) * D
                    last = None
                    for h in range(2):
                        for k in range(NT):
                            last = t.matmul(ps[:, h * NH2:(h + 1) * NH2],
                                            WST[:, wsl + k * 128:wsl + (k + 1) * 128],
                                            opnd_get(k)[:, h * NH2:(h + 1) * NH2],
                                            start=(k == 0), stop=(k == NT - 1))
                    pe.inc(last)
                    if do_stats and m >= 1:
                        j = m - 1
                        wait(t, sq_tok(j), vlast_tok(j))
                        fr = (j == 0)
                        t.matmul(Psa[0:1, :], c_oc16[:, :], XB[:, 0:NH2], start=fr, stop=False)
                        t.matmul(Psb[0:1, :], c_oc16[:, :], XB[:, NH2:SL], start=fr, stop=False)
                        t.matmul(Psc[0:1, :], c_oc16[:, :], SQ[:, 0:NH2], start=fr, stop=False)
                        pe.inc(t.matmul(Psd[0:1, :], c_oc16[:, :], SQ[:, NH2:SL],
                                        start=fr, stop=False))
                if do_stats:
                    j = NT - 1
                    wait(t, sq_tok(j), vlast_tok(j))
                    t.matmul(Psa[0:1, :], c_oc16[:, :], XB[:, 0:NH2], start=False, stop=True)
                    t.matmul(Psb[0:1, :], c_oc16[:, :], XB[:, NH2:SL], start=False, stop=True)
                    t.matmul(Psc[0:1, :], c_oc16[:, :], SQ[:, 0:NH2], start=False, stop=True)
                    pe.inc(t.matmul(Psd[0:1, :], c_oc16[:, :], SQ[:, NH2:SL],
                                    start=False, stop=True))

            @blk.vector
            def _(v):
                wait(v, *BARF)
                for m in range(NT):
                    wait(v, pe_mm(m), xt_tok(m))
                    if m >= 2:
                        wait(v, x1store_tok(m - 2))
                        if do_stats:
                            wait(v, sq_tok(m - 2))
                    sl = (m % 2) * SL
                    inc_chain(v, ve, v.scalar_tensor_tensor(X1[:, sl:sl + SL],
                                             (P_ma if m % 2 == 0 else P_mb)[:, :],
                                             float(half), XT[:, sl:sl + SL],
                                             ALU.mult, ALU.add))
                    if use_bias:
                        inc_chain(v, ve, v.tensor_scalar(X1[:, sl:sl + SL], X1[:, sl:sl + SL],
                                      c_hb[:, m:m + 1], None, ALU.add))
                    if do_stats:
                        if m >= 1:
                            wait(v, pe_st(m - 1))
                        ve.inc(v.tensor_copy(XB[:, :], X1[:, sl:sl + SL]))

            ret_tok = None
            if do_stats:
                @blk.vector
                def _(v):
                    wait(v, pe_st(NT - 1))
                    v.tensor_scalar(r_m[0:1, 0:NH2], Psa[0:1, :], 1.0 / D, None, ALU.mult)
                    v.tensor_scalar(r_m[0:1, NH2:SL], Psb[0:1, :], 1.0 / D, None, ALU.mult)
                    v.tensor_scalar(r_t1[0:1, 0:NH2], Psc[0:1, :], 1.0 / D, None, ALU.mult)
                    ve.inc(v.tensor_scalar(r_t1[0:1, NH2:SL], Psd[0:1, :], 1.0 / D, None, ALU.mult))
                ret_tok = ve.tok()
            return ctx, psx, ret_tok, [st_tok[j] for j in range(NT)]

        if on("p1"):
            with nc.Block() as blk:
                ctx1, psx1, stats2_raw, x1st_tok_by_tile = ffn_pass(
                    blk, bigb_t, ln1_tok, lambda m: strip_view(wffn, m),
                    "sign" if p1_sgn else "cast",
                    xin, XR, p1_stats, 0.5, True)
            psx1.close()
            ctx1.close()

        if on("p2a"):
            with nc.Block() as blk:
                rows2_tok = emit_ln_rows(blk, True, [stats2_raw])

            ctx2a = ExitStack()
            psx2a = ExitStack()
            x1t = ctx2a.enter_context(nc.sbuf_tensor("x1t", [128, 2 * SL], F32))
            tmpa = ctx2a.enter_context(nc.sbuf_tensor("tmpa", [128, SL], F32))
            Pba2 = psx2a.enter_context(nc.psum_tensor("Pba2", [128, SL], F32))
            Pbb2 = psx2a.enter_context(nc.psum_tensor("Pbb2", [128, SL], F32))
            pe0, ve0 = pe.n, ve.n
            BAR2A = bar_toks()
            x1t_tok = {}
            with nc.Block() as blk:
                @blk.sync
                def _(sy):
                    wait(sy, *BAR2A)
                    for k in range(NT):
                        if k >= 2:
                            wait(sy, (ve, ve0 + 3 * (k - 2) + 3))
                        wait(sy, x1st_tok_by_tile[k])
                        x1t_tok[k] = dRH[k % 2].inc(
                            sy.dma_start(out=x1t[:, (k % 2) * SL:(k % 2 + 1) * SL],
                                         in_=XR[k * 128:(k + 1) * 128, :]), 16)

                bc2 = emit_bcast_rows(blk, rows2_tok, Pba2, Pbb2, r_scl, r_mscl,
                                      extra=BAR2A)

                SCB2 = ctx2a.enter_context(nc.sbuf_tensor("scb2", [128, SL], F32))
                MSB2 = ctx2a.enter_context(nc.sbuf_tensor("msb2", [128, SL], F32))

                @blk.vector
                def _(v):
                    wait(v, *BAR2A, bc2)
                    v.tensor_copy(SCB2[:, :], Pba2[:, :])
                    VC(v, v.tensor_copy(MSB2[:, :], Pbb2[:, :]))
                    for k in range(NT):
                        wait(v, x1t_tok[k])
                        sl = (k % 2) * SL
                        inc_chain(v, ve, v.tensor_tensor(tmpa[:, :], x1t[:, sl:sl + SL], SCB2[:, :], ALU.mult))
                        inc_chain(v, ve, v.tensor_tensor(tmpa[:, :], tmpa[:, :], MSB2[:, :], ALU.subtract))
                        inc_chain(v, ve, v.tensor_scalar(biga_t(k), tmpa[:, :], c_lnw[:, k:k + 1], None, ALU.mult))
            xc_tok = ve.tok()
            psx2a.close()
            ctx2a.close()
            if dbg:
                with nc.Block() as blk:
                    @blk.gpsimd
                    def _(g):
                        wait(g, xc_tok)
                        for k in range(NT):
                            dG.inc(g.dma_start(out=dbgt["DXC"][k * 128:(k + 1) * 128, :], in_=biga_t(k)), 16)
                        wait(g, (dS, dS.n))
                        for k in range(NT):
                            dG.inc(g.dma_start(out=dbgt["DX1"][k * 128:(k + 1) * 128, :],
                                               in_=XR[k * 128:(k + 1) * 128, :]), 16)

        if on("p2b"):
            ctx2b = ExitStack()
            psx2b = ExitStack()
            ctxDTR = ExitStack()
            DTR = ctxDTR.enter_context(nc.sbuf_tensor("dtr", [128, SL], F32, side="right"))
            IPS = ctx2b.enter_context(nc.sbuf_tensor("ips", [128, 2 * D], BF16))
            ZT = ctx2b.enter_context(nc.sbuf_tensor("zt", [128, 2 * SL], BF16))
            Pia = psx2b.enter_context(nc.psum_tensor("pia", [128, SL], F32))
            Pib = psx2b.enter_context(nc.psum_tensor("pib", [128, SL], F32))
            pe0, ve0, se0 = pe.n, ve.n, se.n
            ipst_tok = {}
            zst_tok = {}
            evac_tok = []

            def ip_evac_plan(m):
                return (ve, ve0 + m + 1)

            SGZ = ctx2b.enter_context(nc.sbuf_tensor("sgz", [128, 2 * SL], BF16))
            BAR2B = bar_toks()
            with nc.Block() as blk:
                @blk.scalar
                def _(s):
                    wait(s, *BAR2B)
                    load_act(s, "sigmoid_and_others")
                    for m in range(NT):
                        wait(s, (pe, pe0 + m + 1))
                        if m >= 2:
                            wait(s, (ve, ve0 + m - 1))
                        sl = (m % 2) * SL
                        se.inc(s.activation(SGZ[:, sl:sl + SL], (Pia if m % 2 == 0 else Pib)[:, :],
                                            AF.Sigmoid, scale=c_mup[:, m:m + 1]))
                        if m >= 1:
                            wait(s, (ve, ve0 + m))
                            zst_tok[m - 1] = dZ[(m - 1) % 2].inc(
                                s.dma_start(out=ZSP[(m - 1) * 128:m * 128, :],
                                            in_=ZT[:, ((m - 1) % 2) * SL:((m - 1) % 2 + 1) * SL]), 16)

                @blk.scalar
                def _(s):
                    wait(s, (ve, ve0 + NT))
                    zst_tok[NT - 1] = dZ[(NT - 1) % 2].inc(
                        s.dma_start(out=ZSP[(NT - 1) * 128:NT * 128, :],
                                    in_=ZT[:, ((NT - 1) % 2) * SL:((NT - 1) % 2 + 1) * SL]), 16)

                @blk.gpsimd
                def _(g):
                    wait(g, *BAR2B)
                    for m in range(MT_IP):
                        if m >= 2:
                            wait(g, (pe, pe0 + m - 1))
                        ipst_tok[m] = dWg[m % 2].inc(g.dma_start(
                            out=IPS[:, (m % 2) * D:(m % 2 + 1) * D].rearrange("p (k j) -> p k j", k=NT),
                            in_=strip_view(wip, m)), 16)

                @blk.tensor
                def _(t):
                    wait(t, *BAR2B, xc_tok)
                    for m in range(MT_IP):
                        wait(t, ipst_tok[m])
                        if m >= 2:
                            wait(t, ip_evac_plan(m - 2))
                        ps = Pia if m % 2 == 0 else Pib
                        wsl = (m % 2) * D
                        last = None
                        for h in range(2):
                            for k in range(NT):
                                last = t.matmul(ps[:, h * NH2:(h + 1) * NH2],
                                                IPS[:, wsl + k * 128:wsl + (k + 1) * 128],
                                                biga_t(k)[:, h * NH2:(h + 1) * NH2],
                                                start=(k == 0), stop=(k == NT - 1))
                        pe.inc(last)

                @blk.vector
                def _(v):
                    wait(v, *BAR2B)
                    for ct in range(CONVT):
                        ins = v.memset(BIGB[:, ct * XBCW:ct * XBCW + 4], 0)
                    VC(v, ins)
                    for m in range(MT_IP):
                        wait(v, (pe, pe0 + m + 1))
                        ps = (Pia if m % 2 == 0 else Pib)
                        if m < NT:
                            sl = (m % 2) * SL
                            wait(v, (se, se0 + m + 1))
                            if m >= 2:
                                wait(v, zst_tok[m - 2])
                            ve.inc(v.scalar_tensor_tensor(ZT[:, sl:sl + SL], ps[:, :],
                                                          c_mup[:, m:m + 1], SGZ[:, sl:sl + SL],
                                                          ALU.mult, ALU.mult))
                        elif m < NT + CONVT:
                            ct = m - NT
                            ve.inc(v.tensor_scalar(BIGB[:, ct * XBCW + 4:ct * XBCW + 4 + SL],
                                                   ps[:, :], c_mup[:, m:m + 1], None, ALU.mult))
                        else:
                            ve.inc(v.tensor_scalar(DTR[:, :], ps[:, :], c_mup[:, m:m + 1], None, ALU.mult))

            xbc_tok = (ve, ve0 + NT + CONVT)
            dt_tok = (ve, ve0 + MT_IP)
            if not p2b_cc:
                with nc.Block() as blk:
                    @blk.vector
                    def _(v):
                        wait(v, xbc_tok)
                        ve.inc(v.engine_nop())
                halo_tok = ve.tok()
            ctxcc1 = ExitStack()
            HSLOT = ctxcc1.enter_context(nc.sbuf_tensor("hslot", [128, 8 * CONVT * 3], BF16))
            HALO = ctxcc1.enter_context(nc.sbuf_tensor("halo", [128, CONVT * 3], F32))
            TAILS = ctxcc1.enter_context(nc.sbuf_tensor("tails", [128, CONVT * 3], BF16))
            ve1 = ve.n
            with nc.Block() as blk:
                @blk.vector
                def _(v):
                    if not p2b_cc:
                        return
                    wait(v, xbc_tok)
                    ins = None
                    for ct in range(CONVT):
                        ins = v.tensor_copy(TAILS[:, ct * 3:(ct + 1) * 3],
                                            BIGB[:, (ct + 1) * XBCW - 3:(ct + 1) * XBCW])
                    ve.inc(ins)

                @blk.gpsimd
                def _(g):
                    if not p2b_cc:
                        return
                    wait(g, (ve, ve1 + 1))
                    dCC.inc(g.dma_start(out=cc1i.ap(), in_=TAILS[:, :]), 16)
                    wait(g, dCC.tok())
                    ccs.inc(g.collective_compute(
                        "AllGather", ALU.bypass, replica_groups=[list(range(NCORE))],
                        ins=[cc1i.ap()], outs=[cc1o.ap()]), 1)
                    wait(g, (ccs, 1))
                    for j in range(NCORE):
                        dCC.inc(g.dma_start(out=HSLOT[:, j * CONVT * 3:(j + 1) * CONVT * 3],
                                            in_=cc1o.ap()[j * 128:(j + 1) * 128, :]), 16)

                @blk.vector
                def _(v):
                    if not p2b_cc:
                        return
                    wait(v, dCC.tok())
                    VC(v, v.tensor_scalar(HALO[:, :], HSLOT[:, 0:CONVT * 3], c_hsel[:, 0:1], None, ALU.mult))
                    for j in range(1, NCORE):
                        VC(v, v.scalar_tensor_tensor(HALO[:, :], HSLOT[:, j * CONVT * 3:(j + 1) * CONVT * 3],
                                                     c_hsel[:, j:j + 1], HALO[:, :], ALU.mult, ALU.add))
                    for ct in range(CONVT):
                        ins = v.tensor_copy(BIGB[:, ct * XBCW + 1:ct * XBCW + 4],
                                            HALO[:, ct * 3:(ct + 1) * 3])
                        ve.inc(ins)
            if p2b_cc:
                halo_tok = ve.tok()
            ctxcc1.close()
            psx2b.close()
            ctx2b.close()
            if dbg:
                with nc.Block() as blk:
                    @blk.gpsimd
                    def _(g):
                        wait(g, halo_tok)
                        dG.inc(g.dma_start(out=dbgt["DXB"], in_=BIGB[:, :]), 16)

        xsS = lambda c: BIGA[:, c * D:(c + 1) * D]
        if on("p2c"):
            ctxBCB = ExitStack()
            BT16 = ctxBCB.enter_context(nc.sbuf_tensor("bt16", [128, SL], BF16, side="right"))
            CT16 = ctxBCB.enter_context(nc.sbuf_tensor("ct16", [128, SL], BF16, side="right"))
            BS16 = ctxBCB.enter_context(nc.sbuf_tensor("bs16", [128, SL], BF16, side="right"))
            ctx2c = ExitStack()
            psx2c = ExitStack()
            ACC = ctx2c.enter_context(nc.sbuf_tensor("acc", [128, 2 * SL], F32))
            SGT = ctx2c.enter_context(nc.sbuf_tensor("sgt", [128, 2 * SL], BF16))
            XTMP = ctx2c.enter_context(nc.sbuf_tensor("xtmp", [128, 2 * SL], BF16))
            PT0 = psx2c.enter_context(nc.psum_tensor("pt0", [128, SL], BF16))
            PT1 = psx2c.enter_context(nc.psum_tensor("pt1", [128, SL], BF16))
            pe0, ve0, se0 = pe.n, ve.n, se.n
            pe0, ve0, se0 = pe.n, ve.n, se.n
            BAR2C = bar_toks()

            def s_sg(ct):
                return (se, se0 + ct + 1)

            def pe_tr(ct):
                return (pe, pe0 + ct + 1)

            conv_tok = {}
            mult_tok = {}
            evac_tok = {}
            bs_tok = [None]

            with nc.Block() as blk:
                @blk.vector
                def _(v):
                    wait(v, *BAR2C)
                    wait(v, halo_tok)

                    def do_mult(j):
                        wait(v, s_sg(j))
                        jsl = (j % 2) * SL
                        if j < NT:
                            dst = XTMP[:, jsl:jsl + SL]
                        elif j == NT:
                            dst = BT16[:, :]
                        else:
                            dst = CT16[:, :]
                        mult_tok[j] = ve.inc(v.tensor_tensor(dst, ACC[:, jsl:jsl + SL],
                                                             SGT[:, jsl:jsl + SL], ALU.mult))

                    def do_evac(j):
                        wait(v, pe_tr(j))
                        ins = None
                        for c in range(NCH):
                            ins = v.tensor_copy(xsS(c)[:, j * 128:(j + 1) * 128],
                                                (PT0 if j % 2 == 0 else PT1)[:, c * 128:(c + 1) * 128])
                        evac_tok[j] = ve.inc(ins)

                    for ct in range(CONVT):
                        if ct >= 2:
                            wait(v, s_sg(ct - 1))
                            wait(v, mult_tok[ct - 2])
                        sl = (ct % 2) * SL
                        base = ct * XBCW
                        VC(v, v.tensor_scalar(ACC[:, sl:sl + SL], BIGB[:, base + 1:base + 1 + SL],
                                              c_cw[:, ct * 4:ct * 4 + 1], None, ALU.mult))
                        for kk in range(1, 4):
                            op = v.scalar_tensor_tensor(ACC[:, sl:sl + SL],
                                                        BIGB[:, base + 1 + kk:base + 1 + kk + SL],
                                                        c_cw[:, ct * 4 + kk:ct * 4 + kk + 1],
                                                        ACC[:, sl:sl + SL], ALU.mult, ALU.add)
                            if kk < 3:
                                VC(v, op)
                            else:
                                conv_tok[ct] = ve.inc(op)
                                wait(v, conv_tok[ct])
                        if ct >= 1:
                            if ct - 1 >= 2 and ct - 1 < NT:
                                wait(v, pe_tr(ct - 3))
                            do_mult(ct - 1)
                        if ct >= 3 and ct - 3 < NT:
                            do_evac(ct - 3)
                    do_mult(CONVT - 1)
                    for j in range(CONVT - 3, NT):
                        do_evac(j)
                    wait(v, (pe, pe0 + NT + 1))
                    ins = None
                    for c in range(NCH):
                        ins = v.tensor_copy(BS16[:, c * 128:(c + 1) * 128], PT0[:, c * 128:(c + 1) * 128])
                    bs_tok[0] = ve.inc(ins)

                @blk.scalar
                def _(s):
                    wait(s, *BAR2C)
                    for ct in range(CONVT):
                        wait(s, conv_tok[ct])
                        if ct >= 2:
                            wait(s, mult_tok[ct - 2])
                        sl = (ct % 2) * SL
                        se.inc(s.activation(SGT[:, sl:sl + SL], ACC[:, sl:sl + SL], AF.Sigmoid))

                @blk.tensor
                def _(t):
                    wait(t, *BAR2C)
                    for ct in range(NT):
                        wait(t, mult_tok[ct])
                        if ct >= 2:
                            wait(t, evac_tok[ct - 2])
                        sl = (ct % 2) * SL
                        pt = PT0 if ct % 2 == 0 else PT1
                        last = None
                        for c in range(NCH):
                            last = t.transpose(pt[:, c * 128:(c + 1) * 128],
                                               XTMP[:, sl + c * 128:sl + (c + 1) * 128], c_id16[:, :])
                        pe.inc(last)
                    wait(t, mult_tok[NT], evac_tok[NT - 2])
                    last = None
                    for c in range(NCH):
                        last = t.transpose(PT0[:, c * 128:(c + 1) * 128],
                                           BT16[:, c * 128:(c + 1) * 128], c_id16[:, :])
                    pe.inc(last)
            xs_tok = bs_tok[0]
            ct_tok = mult_tok[CONVT - 1]
            psx2c.close()
            ctx2c.close()
            if dbg:
                with nc.Block() as blk:
                    @blk.gpsimd
                    def _(g):
                        wait(g, xs_tok, ct_tok)
                        dG.inc(g.dma_start(out=dbgt["DXS"], in_=BIGA[:, 0:NCH * D]), 16)
                        dG.inc(g.dma_start(out=dbgt["DBT"], in_=BT16[:, :]), 16)
                        dG.inc(g.dma_start(out=dbgt["DCT"], in_=CT16[:, :]), 16)


        if on("p2d"):
            ctxSSD = ExitStack()
            C_bias = ctxSSD.enter_context(nc.sbuf_tensor("c_biasS", [128, NCH * 64], F32))
            C_wscl = ctxSSD.enter_context(nc.sbuf_tensor("c_wsclS", [128, NCH * 64], F32))
            C_dec = ctxSSD.enter_context(nc.sbuf_tensor("c_decS", [128, NCH * 64], F32))
            C_laS = ctxSSD.enter_context(nc.sbuf_tensor("c_laS", [128, NCH * 64], F32))
            HRUN = ctxSSD.enter_context(nc.sbuf_tensor("hrun", [128, D], BF16))
            LSEG = ctxSSD.enter_context(nc.sbuf_tensor("lseg", [64, 1], F32))

            ctx2d = ExitStack()
            psx2d = ExitStack()
            DTF = ctx2d.enter_context(nc.sbuf_tensor("dtf", [64, SL], F32))
            LNDT = ctx2d.enter_context(nc.sbuf_tensor("lndt", [64, SL], F32))
            LAH = ctx2d.enter_context(nc.sbuf_tensor("lah", [64, SL], F32))
            T4 = ctx2d.enter_context(nc.sbuf_tensor("t4_", [64, SL], F32))
            BIASH = ctx2d.enter_context(nc.sbuf_tensor("biash", [64, SL], F32))
            WSH = ctx2d.enter_context(nc.sbuf_tensor("wsh", [64, SL], F32))
            LAE = ctx2d.enter_context(nc.sbuf_tensor("lae", [64, 8], F32))
            DECH = T4
            PTD = psx2d.enter_context(nc.psum_tensor("ptd", [128, 1024], F32))
            BAR2D = bar_toks()
            pe0, ve0, se0 = pe.n, ve.n, se.n
            tokd = {}
            with nc.Block() as blk:
                @blk.scalar
                def _(s):
                    wait(s, *BAR2D)
                    load_act(s, "natural_log_exp_and_others")
                    SC(s, s.activation(LNDT[:, :], DTR[0:64, :], AF.Abs, bias=c_dtb[0:64, :]))
                    SC(s, s.activation(LNDT[:, :], LNDT[:, :], AF.Exp, scale=-1.0))
                    inc_chain(s, se, s.activation(LNDT[:, :], LNDT[:, :], AF.Ln, bias=c_one1[0:64, :]))
                    wait(s, (ve, ve0 + 1))
                    inc_chain(s, se, s.activation(LNDT[:, :], DTF[:, :], AF.Ln))
                    wait(s, (ve, ve0 + 3))
                    inc_chain(s, se, s.activation(WSH[:, :], WSH[:, :], AF.Exp, scale=-1.0))
                    wait(s, (ve, ve0 + 4))
                    inc_chain(s, se, s.activation(LAE[:, :], LAE[:, :], AF.Exp))

                @blk.vector
                def _(v):
                    wait(v, *BAR2D)
                    VC(v, v.tensor_scalar(T4[:, :], DTR[0:64, :], c_dtb[0:64, :], 0.0,
                                          ALU.add, ALU.max))
                    wait(v, (se, se0 + 1))
                    inc_chain(v, ve, v.tensor_tensor(DTF[:, :], T4[:, :], LNDT[:, :], ALU.add))
                    VC(v, v.tensor_scalar(T4[:, :], DTF[:, :], c_acol[0:64, :], None, ALU.mult))
                    for c in range(NCH):
                        VC(v, v.tensor_tensor_scan(LAH[:, c * Q:(c + 1) * Q], T4[:, c * Q:(c + 1) * Q],
                                                   T4[:, c * Q:(c + 1) * Q], 0.0, ALU.add, ALU.bypass))
                    VC(v, v.tensor_copy(LSEG[:, :], LAH[:, Q - 1:Q]))
                    for c in range(1, NCH):
                        VC(v, v.tensor_tensor(LSEG[:, :], LSEG[:, :],
                                              LAH[:, (c + 1) * Q - 1:(c + 1) * Q], ALU.add))
                    wait(v, (se, se0 + 2))
                    inc_chain(v, ve, v.tensor_tensor(BIASH[:, :], LNDT[:, :], LAH[:, :], ALU.subtract))
                    for c in range(NCH):
                        VC(v, v.tensor_scalar(WSH[:, c * Q:(c + 1) * Q], LAH[:, c * Q:(c + 1) * Q],
                                              LAH[:, (c + 1) * Q - 1:(c + 1) * Q], None, ALU.subtract))
                    inc_chain(v, ve, v.tensor_copy(LAE[:, 0:1], LAH[:, Q - 1:Q]))
                    for c in range(1, NCH):
                        VC(v, v.tensor_copy(LAE[:, c:c + 1], LAH[:, (c + 1) * Q - 1:(c + 1) * Q]))
                    inc_chain(v, ve, v.tensor_copy(LAE[:, 0:1], LAH[:, Q - 1:Q]))
                    wait(v, (se, se0 + 3))
                    inc_chain(v, ve, v.tensor_tensor(WSH[:, :], WSH[:, :], DTF[:, :], ALU.mult))
                    wait(v, (se, se0 + 4))
                    for c in range(NCH):
                        VC(v, v.tensor_scalar(DECH[:, c * Q:(c + 1) * Q], c_ones32[0:64, 0:Q],
                                              LAE[:, c:c + 1], None, ALU.mult))
                    inc_chain(v, ve, v.tensor_copy(LAE[:, 0:1], LAE[:, 0:1]))
                    for c in range(NCH):
                        wait(v, (pe, pe0 + c + 1))
                        ph = (c % 2) * 512
                        v.tensor_copy(C_bias[:, c * 64:(c + 1) * 64], PTD[:, ph + 0:ph + 64])
                        v.tensor_copy(C_wscl[:, c * 64:(c + 1) * 64], PTD[:, ph + 64:ph + 128])
                        v.tensor_copy(C_dec[:, c * 64:(c + 1) * 64], PTD[:, ph + 128:ph + 192])
                        ve.inc(v.tensor_copy(C_laS[:, c * 64:(c + 1) * 64], PTD[:, ph + 192:ph + 256]))

                @blk.tensor
                def _(t):
                    wait(t, *BAR2D)
                    wait(t, (ve, ve0 + 6))
                    for c in range(NCH):
                        ph = (c % 2) * 512
                        if c >= 2:
                            wait(t, (ve, ve0 + 6 + (c - 2) + 1))
                        t.transpose(PTD[:, ph + 0:ph + 64], BIASH[0:64, c * Q:(c + 1) * Q], c_id32[0:64, 0:64])
                        t.transpose(PTD[:, ph + 64:ph + 128], WSH[0:64, c * Q:(c + 1) * Q], c_id32[0:64, 0:64])
                        t.transpose(PTD[:, ph + 128:ph + 192], DECH[0:64, c * Q:(c + 1) * Q], c_id32[0:64, 0:64])
                        pe.inc(t.transpose(PTD[:, ph + 192:ph + 256], LAH[0:64, c * Q:(c + 1) * Q],
                                           c_id32[0:64, 0:64]))
            assert ve.n == ve0 + 6 + NCH, (ve.n, ve0)
            assert se.n == se0 + 4
            assert pe.n == pe0 + NCH
            prep_tok = ve.tok()
            if dbg:
                with nc.Block() as blk:
                    @blk.gpsimd
                    def _(g):
                        wait(g, prep_tok)
                        dG.inc(g.dma_start(out=dbgt["DDT"][0:64, :], in_=DTF[:, :]), 16)
                        dG.inc(g.dma_start(out=dbgt["DLA"][0:64, :], in_=LAH[:, :]), 16)
            psx2d.close()
            ctx2d.close()

        if on("p2e"):
            ctx2e = ExitStack()
            psx2e = ExitStack()
            HZ = ctx2e.enter_context(nc.sbuf_tensor("hz", [128, D], BF16))
            XH = ctx2e.enter_context(nc.sbuf_tensor("xh", [128, 4 * 64], BF16))
            LSEGA = ctx2e.enter_context(nc.sbuf_tensor("lsega", [8, NHD], F32))
            W8 = ctx2e.enter_context(nc.sbuf_tensor("w8", [8, NHD], F32))
            WFLAT = ctx2e.enter_context(nc.sbuf_tensor("wflat", [1, 512], F32))
            WBC = ctx2e.enter_context(nc.sbuf_tensor("wbc", [128, 512], F32))
            HZS = ctx2e.enter_context(nc.sbuf_tensor("hzs", [128, D], BF16))
            PH1 = psx2e.enter_context(nc.psum_tensor("ph1", [128, 2048], F32))
            PT8 = psx2e.enter_context(nc.psum_tensor("pt8", [8, NHD], F32))
            PW = psx2e.enter_context(nc.psum_tensor("pw", [128, 512], F32))
            BAR2E = bar_toks()
            pe0, ve0 = pe.n, ve.n
            XA1 = 4
            NTOT = NCH * NHD
            xh_tok = {}
            hz_tok = {}
            _n = ve0
            for i0 in range(XA1):
                _n += 1; xh_tok[i0] = (ve, _n)
            for i0 in range(NTOT):
                if i0 + XA1 < NTOT:
                    _n += 1; xh_tok[i0 + XA1] = (ve, _n)
                _n += 1; hz_tok[i0] = (ve, _n)
            with nc.Block() as blk:
                @blk.vector
                def _(v):
                    wait(v, *BAR2E, prep_tok)
                    VC(v, v.memset(HZ[:, :], 0))

                    def em_xh(i):
                        c, r = divmod(i, NHD)
                        if i >= 4:
                            wait(v, (pe, pe0 + i - 3))
                        ins = v.tensor_scalar(XH[:, (i % 4) * 64:(i % 4) * 64 + 64],
                                              xsS(c)[:, r * 64:(r + 1) * 64],
                                              C_wscl[:, c * 64 + r:c * 64 + r + 1], None, ALU.mult)
                        assert ve.inc(ins)[1] == xh_tok[i][1]

                    for i0 in range(XA1):
                        em_xh(i0)
                    for i in range(NTOT):
                        c, r = divmod(i, NHD)
                        if i + XA1 < NTOT:
                            em_xh(i + XA1)
                        wait(v, (pe, pe0 + i + 1))
                        if c >= 1:
                            wait(v, hz_tok[i - NHD])
                        ins = v.scalar_tensor_tensor(HZ[:, r * 64:(r + 1) * 64],
                                                     HZ[:, r * 64:(r + 1) * 64],
                                                     C_dec[:, c * 64 + r:c * 64 + r + 1],
                                                     PH1[:, (i % 4) * 512:(i % 4) * 512 + 64],
                                                     ALU.mult, ALU.add)
                        assert ve.inc(ins)[1] == hz_tok[i][1]

                @blk.tensor
                def _(t):
                    wait(t, *BAR2E)
                    for c in range(NCH):
                        for r in range(NHD):
                            i = c * NHD + r
                            wait(t, xh_tok[i])
                            if i >= 4:
                                wait(t, hz_tok[i - 4])
                            pe.inc(t.matmul(PH1[:, (i % 4) * 512:(i % 4) * 512 + 64],
                                            BS16[:, c * 128:(c + 1) * 128],
                                            XH[:, (i % 4) * 64:(i % 4) * 64 + 64],
                                            start=True, stop=True))
            assert pe.n == pe0 + NCH * NHD
            assert ve.n == _n
            hz_done = hz_tok[NCH * NHD - 1]

            ve1 = ve.n
            with nc.Block() as blk:
                @blk.gpsimd
                def _(g):
                    wait(g, hz_done, prep_tok)
                    dCC.inc(g.dma_start(out=cc2hi.ap(), in_=HZ[:, :]), 16)
                    dCC.inc(g.dma_start(out=cc2li.ap(), in_=LSEG[:, :]), 16)
                    wait(g, dCC.tok())
                    ccs.inc(g.collective_compute(
                        "AllGather", ALU.bypass, replica_groups=[list(range(NCORE))],
                        ins=[cc2hi.ap()], outs=[cc2ho.ap()]), 1)
                    ccs.inc(g.collective_compute(
                        "AllGather", ALU.bypass, replica_groups=[list(range(NCORE))],
                        ins=[cc2li.ap()], outs=[cc2lo.ap()]), 1)
                    wait(g, ccs.tok())
                    dCC.inc(g.dma_start(out=LSEGA[:, :], in_=cc2lo.ap()), 16)
            lsega_tok = dCC.tok()

            pe1, se1 = pe.n, se.n
            with nc.Block() as blk:
                @blk.tensor
                def _(t):
                    wait(t, lsega_tok)
                    pe.inc(t.matmul(PT8[:, :], c_selr2[:, :], LSEGA[:, :], start=True, stop=True))

                @blk.scalar
                def _(s):
                    wait(s, (pe, pe1 + 1))
                    se.inc(s.activation(W8[:, :], PT8[:, :], AF.Exp))

                @blk.vector
                def _(v):
                    wait(v, (se, se1 + 1))
                    ve.inc(v.tensor_tensor(W8[:, :], W8[:, :], c_pfx8[:, :], ALU.mult))
            w8_tok = ve.tok()

            ve2, pe2 = ve.n, pe.n
            hzs_tok = {}
            wflat_tok = [None]
            with nc.Block() as blk:
                @blk.gpsimd
                def _(g):
                    wait(g, w8_tok)
                    wflat_tok[0] = dCC.inc(g.dma_start(out=WFLAT[0:1, :], in_=W8[:, :]), 16)
                    for j in range(NCORE):
                        if j >= 1:
                            wait(g, (ve, ve2 + 1 + j * NHD))
                        hzs_tok[j] = dX[j % 2].inc(
                            g.dma_start(out=HZS[:, :],
                                        in_=cc2ho.ap()[j * 128:(j + 1) * 128, :]), 16)

                @blk.tensor
                def _(t):
                    wait(t, wflat_tok[0])
                    last = None
                    for j in range(NCORE):
                        last = t.matmul(PW[:, j * 64:(j + 1) * 64], c_or32[0:1, :],
                                        WFLAT[0:1, j * 64:(j + 1) * 64], start=True, stop=True)
                    pe.inc(last)

                @blk.vector
                def _(v):
                    wait(v, (pe, pe2 + 1))
                    inc_chain(v, ve, v.tensor_copy(WBC[:, :], PW[:, :]))
                    for j in range(NCORE):
                        wait(v, hzs_tok[j])
                        if j >= 1:
                            wait(v, (ve, ve2 + 1 + j * NHD))
                        for r in range(NHD):
                            dst = HRUN[:, r * 64:(r + 1) * 64]
                            src = HZS[:, r * 64:(r + 1) * 64]
                            wcol = WBC[:, j * 64 + r:j * 64 + r + 1]
                            if j == 0:
                                ins = v.tensor_scalar(dst, src, wcol, None, ALU.mult)
                            else:
                                ins = v.scalar_tensor_tensor(dst, src, wcol, dst, ALU.mult, ALU.add)
                            ve.inc(ins)
            assert ve.n == ve2 + 1 + NCORE * NHD
            hin_tok = ve.tok()
            if dbg:
                with nc.Block() as blk:
                    @blk.gpsimd
                    def _(g):
                        wait(g, hz_done)
                        dG.inc(g.dma_start(out=dbgt["DHZ"], in_=HZ[:, :]), 16)
                        wait(g, hin_tok)
                        dG.inc(g.dma_start(out=dbgt["DHI"], in_=HRUN[:, :]), 16)
            psx2e.close()
            ctx2e.close()

        if on("p2f"):
            ctx2f = ExitStack()
            psx2f = ExitStack()
            GM16 = ctx2f.enter_context(nc.sbuf_tensor("gm16", [128, 128], BF16))
            GM2 = ctx2f.enter_context(nc.sbuf_tensor("gm2", [128, 256], BF16))
            CT2 = ctx2f.enter_context(nc.sbuf_tensor("ct2", [128, 256], BF16))
            LACB = ctx2f.enter_context(nc.sbuf_tensor("lacb", [128, 4 * 128], F32))
            DM = ctx2f.enter_context(nc.sbuf_tensor("dm", [128, 4 * 128], F32))
            E16 = BIGB[:, NT * SL + 1024:NT * SL + 1536]
            EL16 = BIGB[:, NT * SL + 1536:NT * SL + 2048]
            MT16 = ctx2f.enter_context(nc.sbuf_tensor("mt16", [128, 4 * 128], BF16))
            CH16 = ctx2f.enter_context(nc.sbuf_tensor("ch16", [128, 4 * 128], BF16))
            XH2 = ctx2f.enter_context(nc.sbuf_tensor("xh2", [128, 4 * 64], BF16))
            G16 = ctx2f.enter_context(nc.sbuf_tensor("g16", [128, NT * 128], BF16))
            GSQ = BIGB[:, NT * SL:NT * SL + 1024]
            SZC = ctx2f.enter_context(nc.sbuf_tensor("szc", [128, NT * 128], BF16))
            RMSR = ctx2f.enter_context(nc.sbuf_tensor("rmsr", [1, 128], F32))
            PGX = psx2f.enter_context(nc.psum_tensor("pgx", [128, 512], F32))
            PD2 = psx2f.enter_context(nc.psum_tensor("pd2", [128, 3 * 512], F32))
            PY = psx2f.enter_context(nc.psum_tensor("py", [128, 1024], F32))
            PH2 = psx2f.enter_context(nc.psum_tensor("ph2", [128, 1024], F32))
            BAR2F = bar_toks()
            pe0, ve0, se0 = pe.n, ve.n, se.n
            PG = PGX[:, 0:128]
            PRB = PGX[:, 128:256]
            PSS = PGX[0:1, 256:384]

            PE_PC = 1 + NHD * 3 + 5
            SE_PC = 2 * NHD + 5
            VEH = 7
            VPAIR = 2 * VEH + 1
            VE_PC = 1 + (NHD // 2) * VPAIR + NT

            def pe_gcb(c):
                return (pe, pe0 + c * PE_PC + 1)

            def pe_pdtr(c, r):
                return (pe, pe0 + c * PE_PC + 1 + 3 * r + 1)

            def pe_y(c, r):
                return (pe, pe0 + c * PE_PC + 1 + 3 * r + 2)

            def pe_hc(c, r):
                return (pe, pe0 + c * PE_PC + 1 + 3 * r + 3)

            def pe_ssk(c, k):
                return (pe, pe0 + c * PE_PC + 1 + 3 * NHD + 1 + k)

            def pe_ss(c):
                return (pe, pe0 + c * PE_PC + 1 + 3 * NHD + 4)

            def pe_rmsbc(c):
                return (pe, pe0 + c * PE_PC + PE_PC)

            def se_edm(c, r):
                return (se, STOK[("edm", c, r)])

            def se_ela(c, r):
                return (se, STOK[("ela", c, r)])

            def se_gsqk(c, k):
                return (se, STOK[("gsq", c, k)])

            def se_rms(c):
                return (se, STOK[("rms", c)])

            def vh(c, r, o):
                return (ve, ve0 + c * VE_PC + 1 + (r // 2) * VPAIR + (r % 2) * VEH + o)

            def ve_gm(c):
                return (ve, ve0 + c * VE_PC + 1)

            def ve_lacb(c, r):
                return vh(c, r, 1)

            def ve_dm(c, r):
                return vh(c, r, 2)

            def ve_mta(c, r):
                return vh(c, r, 3)

            def ve_mtb(c, r):
                return vh(c, r, 4)

            def ve_ch(c, r):
                return vh(c, r, 5)

            def ve_xh(c, r):
                return vh(c, r, 6)

            def ve_hup(c, r):
                return vh(c, r, 7)

            def ve_gev(c, q):
                return (ve, ve0 + c * VE_PC + 1 + (q + 1) * VPAIR)

            def ve_yn(c):
                return (ve, ve0 + (c + 1) * VE_PC)

            LA_A, DM_A, MT_A = 6, 4, 1
            PDTR_A = 5

            def q_pair(r):
                return r // 2

            def plan_vector():
                n = ve0
                toks = {}
                for c in range(NCH):
                    n += 1
                    toks[("gm", c)] = n
                    n += 2
                    toks[("gm2", c)] = n
                    for r0 in range(DM_A):
                        n += 1
                        toks[("dm", c, r0)] = n
                    n += 1; toks[("mta", c, 0)] = n
                    n += 1; toks[("mtb", c, 0)] = n
                    n += 1; toks[("mtb", c, 1)] = n
                    n += 1; toks[("ch", c, 0)] = n
                    for r in range(NHD):
                        if r + DM_A < NHD:
                            n += 1; toks[("dm", c, r + DM_A)] = n
                        if r % 2 == 1 and (r + 1) // 2 < NHD // 2:
                            qn = (r + 1) // 2
                            n += 1; toks[("mta", c, qn)] = n
                            n += 1; toks[("mtb", c, 2 * qn)] = n
                            n += 1; toks[("mtb", c, 2 * qn + 1)] = n
                            n += 1; toks[("ch", c, qn)] = n
                        n += 1; toks[("hup", c, r)] = n
                        if r % 2 == 1:
                            n += 1; toks[("gev", c, r // 2)] = n
                    for k in range(NT):
                        n += 1
                    toks[("yn", c)] = n
                return toks, n

            def plan_tensor():
                n = pe0
                toks = {}
                for c in range(NCH):
                    n += 1; toks[("gcb", c)] = n
                    for r0 in range(PDTR_A):
                        n += 1; toks[("pdtr", c, r0)] = n
                    for r in range(NHD):
                        if r + PDTR_A < NHD:
                            n += 1; toks[("pdtr", c, r + PDTR_A)] = n
                        n += 1; toks[("y", c, r)] = n
                        n += 1; toks[("hc", c, r)] = n
                    for qq in range(4):
                        n += 1; toks[("ssk", c, qq)] = n
                    n += 1; toks[("rmsbc", c)] = n
                return toks, n

            def plan_scalar():
                n = se0
                toks = {}
                for c in range(NCH):
                    for r0 in range(LA_A):
                        n += 1; toks[("lacb", c, r0)] = n
                    n += 1; toks[("xh", c, 0)] = n
                    for r in range(NHD):
                        if r + LA_A < NHD:
                            n += 1; toks[("lacb", c, r + LA_A)] = n
                        n += 1; toks[("edm", c, r)] = n
                        n += 1; toks[("ela", c, r)] = n
                        if r + 1 < NHD:
                            n += 1; toks[("xh", c, r + 1)] = n
                    for qq in range(4):
                        n += 1; toks[("gsq", c, qq)] = n
                    n += 1; toks[("rms", c)] = n
                return toks, n

            VTOK, ve_end = plan_vector()
            TTOK, pe_end = plan_tensor()
            STOK, se_end = plan_scalar()

            def st_(*k):
                return (se, STOK[k])

            def vt(*k):
                return (ve, VTOK[k])

            def tt(*k):
                return (pe, TTOK[k])

            def se_edm(c, r):
                return (se, STOK[("edm", c, r)])

            def se_ela(c, r):
                return (se, STOK[("ela", c, r)])

            def se_gsqk(c, k):
                return (se, STOK[("gsq", c, k)])

            def se_rms(c):
                return (se, STOK[("rms", c)])

            szc_tok = {}
            with nc.Block() as blk:
                @blk.gpsimd
                def _(g):
                    wait(g, *BAR2F)
                    for c in range(NCH):
                        if c >= 1:
                            wait(g, (ve, VTOK[("gev", c - 1, NHD // 2 - 1)]))
                        wait(g, zst_tok[NT - 2], zst_tok[NT - 1])
                        szc_tok[c] = dR[c % 2].inc(g.dma_start(
                            out=SZC[:, :].rearrange("p (k t) -> p k t", k=NT),
                            in_=ZSP.rearrange("(k p) t -> p k t", p=128)[:, :, c * 128:(c + 1) * 128]), 16)

                @blk.vector
                def _(v):
                    wait(v, *BAR2F)

                    def em_dm(c, r):
                        i = c * NHD + r
                        wait(v, tt("pdtr", c, r))
                        if i >= 4:
                            pr = i - 4
                            wait(v, se_edm(pr // NHD, pr % NHD))
                        ins = v.tensor_scalar(DM[:, (i % 4) * 128:(i % 4 + 1) * 128],
                                              PD2[:, (i % 3) * 512:(i % 3) * 512 + 128],
                                              C_bias[:, c * 64 + r:c * 64 + r + 1],
                                              60.0, ALU.add, ALU.min)
                        assert ve.inc(ins)[1] == VTOK[("dm", c, r)]

                    def em_pair(c, qn):
                        r0 = 2 * qn
                        i = c * NHD + r0
                        i4 = i % 4
                        wait(v, se_edm(c, r0 + 1))
                        if i >= 4:
                            pr = i - 4
                            wait(v, tt("y", pr // NHD, pr % NHD + 1))
                        ins = v.tensor_tensor(MT16[:, i4 * 128:(i4 + 2) * 128],
                                              E16[:, i4 * 128:(i4 + 2) * 128],
                                              GM2[:, :], ALU.mult)
                        assert ve.inc(ins)[1] == VTOK[("mta", c, qn)]
                        wait(v, (ve, VTOK[("mta", c, qn)]))
                        for rr in (r0, r0 + 1):
                            j4 = (c * NHD + rr) % 4
                            ins = v.scalar_tensor_tensor(MT16[:, j4 * 128:(j4 + 1) * 128],
                                                         c_id16[:, :], c_drep[:, rr:rr + 1],
                                                         MT16[:, j4 * 128:(j4 + 1) * 128],
                                                         ALU.mult, ALU.add)
                            assert ve.inc(ins)[1] == VTOK[("mtb", c, rr)]
                        wait(v, se_ela(c, r0 + 1))
                        ins = v.tensor_tensor(CH16[:, i4 * 128:(i4 + 2) * 128],
                                              CT2[:, :],
                                              EL16[:, i4 * 128:(i4 + 2) * 128], ALU.mult)
                        assert ve.inc(ins)[1] == VTOK[("ch", c, qn)]


                    for c in range(NCH):
                        wait(v, tt("gcb", c))
                        if c >= 1:
                            wait(v, (ve, VTOK[("yn", c - 1)]))
                        ins = v.tensor_tensor(GM16[:, :], PG, c_mask01[:, :], ALU.mult)
                        assert ve.inc(ins)[1] == VTOK[("gm", c)]
                        wait(v, (ve, VTOK[("gm", c)]))
                        v.tensor_copy(GM2[:, 0:128], GM16[:, :])
                        ve.inc(v.tensor_copy(GM2[:, 128:256], GM16[:, :]))
                        VC(v, v.tensor_copy(CT2[:, 0:128], CT16[:, c * 128:(c + 1) * 128]))
                        ins = v.tensor_copy(CT2[:, 128:256], CT16[:, c * 128:(c + 1) * 128])
                        assert ve.inc(ins)[1] == VTOK[("gm2", c)]
                        wait(v, (ve, VTOK[("gm2", c)]))
                        for r0 in range(DM_A):
                            em_dm(c, r0)
                        em_pair(c, 0)
                        for r in range(NHD):
                            if r + DM_A < NHD:
                                em_dm(c, r + DM_A)
                            if r % 2 == 1 and (r + 1) // 2 < NHD // 2:
                                em_pair(c, (r + 1) // 2)
                            i = c * NHD + r
                            if c == 0 and r == 0:
                                wait(v, hin_tok)
                            wait(v, tt("hc", c, r))
                            if c >= 1:
                                wait(v, (ve, VTOK[("hup", c - 1, r)]))
                            ins = v.scalar_tensor_tensor(HRUN[:, r * 64:(r + 1) * 64],
                                                         HRUN[:, r * 64:(r + 1) * 64],
                                                         C_dec[:, c * 64 + r:c * 64 + r + 1],
                                                         PH2[:, (i % 2) * 512:(i % 2) * 512 + 64],
                                                         ALU.mult, ALU.add)
                            assert ve.inc(ins)[1] == VTOK[("hup", c, r)]
                            if r % 2 == 1:
                                q = r // 2
                                wait(v, tt("y", c, r), szc_tok[c])
                                ins = v.tensor_tensor(G16[:, q * 128:(q + 1) * 128],
                                                      PY[:, ((i // 2) % 2) * 512:((i // 2) % 2) * 512 + 128],
                                                      SZC[:, q * 128:(q + 1) * 128], ALU.mult)
                                assert ve.inc(ins)[1] == VTOK[("gev", c, q)]
                        wait(v, tt("rmsbc", c))
                        last_v = None
                        for k in range(NT):
                            last_v = ve.inc(v.scalar_tensor_tensor(
                                BIGB[:, k * SL + c * 128:k * SL + (c + 1) * 128],
                                G16[:, k * 128:(k + 1) * 128],
                                c_nrmw[:, k:k + 1],
                                PRB, ALU.mult, ALU.mult))[1]
                        assert last_v == VTOK[("yn", c)]

                @blk.scalar
                def _(s):
                    wait(s, *BAR2F)

                    def em_lacb_s(c, r):
                        i = c * NHD + r
                        if i >= 4:
                            pr = i - 4
                            wait(s, tt("pdtr", pr // NHD, pr % NHD))
                        ins = s.activation(LACB[:, (i % 4) * 128:(i % 4 + 1) * 128],
                                           c_ones32[:, :], AF.Copy,
                                           scale=C_laS[:, c * 64 + r:c * 64 + r + 1])
                        assert se.inc(ins)[1] == STOK[("lacb", c, r)]

                    def em_xh_s(c, r):
                        i = c * NHD + r
                        if i >= 4:
                            pr = i - 4
                            wait(s, tt("hc", pr // NHD, pr % NHD))
                        ins = s.activation(XH2[:, (i % 4) * 64:(i % 4) * 64 + 64],
                                           xsS(c)[:, r * 64:(r + 1) * 64], AF.Copy,
                                           scale=C_wscl[:, c * 64 + r:c * 64 + r + 1])
                        assert se.inc(ins)[1] == STOK[("xh", c, r)]

                    for c in range(NCH):
                        for r0 in range(LA_A):
                            em_lacb_s(c, r0)
                        em_xh_s(c, 0)
                        for r in range(NHD):
                            i = c * NHD + r
                            i4 = i % 4
                            i3 = i % 3
                            if r + LA_A < NHD:
                                em_lacb_s(c, r + LA_A)
                            wait(s, (ve, VTOK[("dm", c, r)]))
                            if i >= 4:
                                pr = i - 4
                                wait(s, (ve, VTOK[("mta", pr // NHD, (pr % NHD) // 2)]))
                            se.inc(s.activation(E16[:, i4 * 128:(i4 + 1) * 128],
                                                DM[:, i4 * 128:(i4 + 1) * 128], AF.Exp))
                            wait(s, tt("pdtr", c, r))
                            if i >= 4:
                                pr = i - 4
                                wait(s, (ve, VTOK[("ch", pr // NHD, (pr % NHD) // 2)]))
                            se.inc(s.activation(EL16[:, i4 * 128:(i4 + 1) * 128],
                                                PD2[:, i3 * 512:i3 * 512 + 128], AF.Exp))
                            if r + 1 < NHD:
                                em_xh_s(c, r + 1)
                        QW = NT * 32
                        wait(s, (ve, VTOK[("gev", c, NHD // 2 - 1)]))
                        for qq in range(4):
                            if qq >= 1:
                                wait(s, tt("ssk", c, qq - 1))
                            ins = s.activation(GSQ[:, :], G16[:, qq * QW:(qq + 1) * QW], AF.Square)
                            assert se.inc(ins)[1] == STOK[("gsq", c, qq)]
                        wait(s, tt("ssk", c, 3))
                        SC(s, s.activation(RMSR[0:1, :], PSS, AF.Ln,
                                           bias=c_eps1[0:1, :], scale=1.0 / D))
                        ins = s.activation(RMSR[0:1, :], RMSR[0:1, :], AF.Exp, scale=-0.5)
                        assert se.inc(ins)[1] == STOK[("rms", c)]

                @blk.tensor
                def _(t):
                    wait(t, *BAR2F)

                    def em_pdtr(c, r):
                        i = c * NHD + r
                        wait(t, st_("lacb", c, r))
                        if i >= 3:
                            pr = i - 3
                            wait(t, se_ela(pr // NHD, pr % NHD), vt("dm", pr // NHD, pr % NHD))
                        ins = t.transpose(PD2[:, (i % 3) * 512:(i % 3) * 512 + 128],
                                          LACB[:, (i % 4) * 128:(i % 4 + 1) * 128], c_id32[:, :])
                        assert pe.inc(ins)[1] == TTOK[("pdtr", c, r)]

                    for c in range(NCH):
                        if c >= 1:
                            wait(t, vt("yn", c - 1))
                        ins = t.matmul(PG, BT16[:, c * 128:(c + 1) * 128],
                                       CT16[:, c * 128:(c + 1) * 128], start=True, stop=True)
                        assert pe.inc(ins)[1] == TTOK[("gcb", c)]
                        for r0 in range(PDTR_A):
                            em_pdtr(c, r0)
                        for r in range(NHD):
                            i = c * NHD + r
                            i4 = i % 4
                            if r + PDTR_A < NHD:
                                em_pdtr(c, r + PDTR_A)
                            if c == 0 and r == 0:
                                wait(t, hin_tok)
                            wait(t, vt("mtb", c, r), vt("ch", c, q_pair(r)))
                            if (i // 2) >= 2 and r % 2 == 0:
                                pq = (i // 2) - 2
                                wait(t, vt("gev", pq // (NHD // 2), pq % (NHD // 2)))
                            pb = (r % 2) * 64
                            tp = (0, pb)
                            ysl = ((i // 2) % 2) * 512
                            t.matmul(PY[pb:pb + 64, ysl:ysl + 128],
                                     xsS(c)[:, r * 64:(r + 1) * 64],
                                     MT16[:, i4 * 128:(i4 + 1) * 128],
                                     start=True, stop=False, tile_position=tp)
                            ins = t.matmul(PY[pb:pb + 64, ysl:ysl + 128],
                                           HRUN[:, r * 64:(r + 1) * 64],
                                           CH16[:, i4 * 128:(i4 + 1) * 128],
                                           start=False, stop=True, tile_position=tp)
                            assert pe.inc(ins)[1] == TTOK[("y", c, r)]
                            wait(t, st_("xh", c, r))
                            if i >= 2:
                                pr = i - 2
                                wait(t, vt("hup", pr // NHD, pr % NHD))
                            ins = t.matmul(PH2[:, (i % 2) * 512:(i % 2) * 512 + 64],
                                           BS16[:, c * 128:(c + 1) * 128],
                                           XH2[:, i4 * 64:(i4 + 1) * 64],
                                           start=True, stop=True)
                            assert pe.inc(ins)[1] == TTOK[("hc", c, r)]
                        last = None
                        for qq in range(4):
                            wait(t, se_gsqk(c, qq))
                            for k in range(NT // 4):
                                last = t.matmul(PSS, c_oc16[:, :], GSQ[:, k * 128:(k + 1) * 128],
                                                start=(qq == 0 and k == 0),
                                                stop=(qq == 3 and k == NT // 4 - 1))
                            assert pe.inc(last)[1] == TTOK[("ssk", c, qq)]
                        wait(t, se_rms(c), vt("gm", c))
                        ins = t.matmul(PRB, c_or32[0:1, :], RMSR[0:1, :], start=True, stop=True)
                        assert pe.inc(ins)[1] == TTOK[("rmsbc", c)]
            assert pe.n == pe_end
            assert se.n == se_end
            assert ve.n == ve_end
            yn_tok = ve.tok()
            if dbg:
                with nc.Block() as blk:
                    @blk.gpsimd
                    def _(g):
                        wait(g, yn_tok)
                        dG.inc(g.dma_start(out=dbgt["DYN"], in_=BIGB[:, 0:NT * SL]), 16)
            psx2f.close()
            ctx2f.close()

        if on("p2d"):
            ctxSSD.close()
        if on("p2c"):
            ctxBCB.close()
        if on("p2b"):
            ctxDTR.close()

        if on("p2h"):
            with nc.Block() as blk:
                ctxh, psxh, stats3_raw, x2st_tok_by_tile = ffn_pass(
                    blk, lambda k: BIGB[:, k * SL:(k + 1) * SL], yn_tok,
                    lambda m: strip_view(wop, m), "cast",
                    XR, XR, True, 1.0, False)
            psxh.close()
            ctxh.close()

        if on("p3"):
            with nc.Block() as blk:
                rows3_tok = emit_ln_rows(blk, False, [stats3_raw])

            ctx3 = ExitStack()
            psx3 = ExitStack()
            x2t = ctx3.enter_context(nc.sbuf_tensor("x2t", [128, 2 * SL], F32))
            tmp3 = ctx3.enter_context(nc.sbuf_tensor("tmp3", [128, SL], F32))
            SCB3 = ctx3.enter_context(nc.sbuf_tensor("scb3", [128, SL], F32))
            MSB3 = ctx3.enter_context(nc.sbuf_tensor("msb3", [128, SL], F32))
            Pba3 = psx3.enter_context(nc.psum_tensor("Pba3", [128, SL], F32))
            Pbb3 = psx3.enter_context(nc.psum_tensor("Pbb3", [128, SL], F32))
            pe0, ve0 = pe.n, ve.n
            BAR3 = bar_toks()
            x2t_tok = {}
            with nc.Block() as blk:
                @blk.sync
                def _(sy):
                    wait(sy, *BAR3)
                    for k in range(NT):
                        if k >= 2:
                            wait(sy, (ve, ve0 + 3 * (k - 2) + 3))
                        wait(sy, x2st_tok_by_tile[k])
                        x2t_tok[k] = dRH[k % 2].inc(
                            sy.dma_start(out=x2t[:, (k % 2) * SL:(k % 2 + 1) * SL],
                                         in_=XR[k * 128:(k + 1) * 128, :]), 16)

                bc3 = emit_bcast_rows(blk, rows3_tok, Pba3, Pbb3, r_scl, r_mscl,
                                      extra=BAR3)

                @blk.vector
                def _(v):
                    wait(v, *BAR3, bc3)
                    v.tensor_copy(SCB3[:, :], Pba3[:, :])
                    VC(v, v.tensor_copy(MSB3[:, :], Pbb3[:, :]))
                    for k in range(NT):
                        wait(v, x2t_tok[k])
                        sl = (k % 2) * SL
                        inc_chain(v, ve, v.tensor_tensor(tmp3[:, :], x2t[:, sl:sl + SL],
                                                         SCB3[:, :], ALU.mult))
                        inc_chain(v, ve, v.tensor_tensor(tmp3[:, :], tmp3[:, :],
                                                         MSB3[:, :], ALU.subtract))
                        inc_chain(v, ve, v.tensor_copy(biga_t(k), tmp3[:, :]))
            ln3_tok = ve.tok()
            psx3.close()
            ctx3.close()

            with nc.Block() as blk:
                ctxF, psxF, _, _ = ffn_pass(
                    blk, biga_t, ln3_tok,
                    sgn_strip_view, "sgn16",
                    XR, out_d, False, 0.5, True)
            psxF.close()
            ctxF.close()

        with nc.Block() as blk:
            @blk.sync
            def _(sy):
                wait(sy, *bar_toks())
                if limit_i < PHASES.index("p3"):
                    for m in range(NT):
                        fin.inc(sy.dma_start(out=out_d[m * 128:(m + 1) * 128, :],
                                             in_=XR[m * 128:(m + 1) * 128, :]), 16)
                wait(sy, fin.tok())
    return nc


def make_in_maps(inputs):
    import ml_dtypes
    _ = np.asarray(inputs["L0_attn_residual"]).shape
    x = np.ascontiguousarray(np.asarray(inputs["L0_attn_hidden_states"], dtype=np.float32)[0])
    Wf = np.ascontiguousarray(np.asarray(inputs["L0_ffn_weight"], dtype=np.float32))
    bf = np.asarray(inputs["L0_ffn_bias"], dtype=np.float32)
    lnw = np.asarray(inputs["L0_attn_ln_w"], dtype=np.float32)
    Wip = np.ascontiguousarray(np.asarray(inputs["L0_attn_in_proj_w"], dtype=np.float32))
    cw = np.asarray(inputs["L0_attn_conv_w"], dtype=np.float32)
    alog = np.asarray(inputs["L0_attn_A_log"], dtype=np.float32)
    dtb = np.asarray(inputs["L0_attn_dt_bias"], dtype=np.float32)
    Dv = np.asarray(inputs["L0_attn_D"], dtype=np.float32)
    nrmw = np.asarray(inputs["L0_attn_norm_w"], dtype=np.float32)
    Wop = np.ascontiguousarray(np.asarray(inputs["L0_attn_out_proj_w"], dtype=np.float32))
    mup = np.asarray(inputs["L0_attn_mup_vector"], dtype=np.float32)

    xT = np.ascontiguousarray(x.T)
    wffn_t = np.ascontiguousarray(Wf.T)
    wip_pad = np.zeros((D, PROJ_PAD), np.float32)
    wip_pad[:, :8512] = Wip
    mup_pad = np.zeros(PROJ_PAD, np.float32)
    mup_pad[:8512] = mup

    cols = lambda vec, nt: np.ascontiguousarray(vec.reshape(nt, 128).T)
    dtb_c = np.zeros((128, 1), np.float32); dtb_c[:64, 0] = dtb
    alog_c = np.zeros((128, 1), np.float32); alog_c[:64, 0] = alog
    s_idx = np.arange(128)[:, None]
    t_idx = np.arange(128)[None, :]

    shared = dict(
        wffn=wffn_t, wip=wip_pad, wop=Wop,
        lnw_c=cols(lnw, NT), lnw_r=np.ascontiguousarray(lnw.reshape(1, D)),
        nrmw_c=cols(nrmw, NT), bias_c=cols(bf, NT),
        mup_c=np.ascontiguousarray(mup_pad.reshape(MT_IP, 128).T),
        cw_c=np.ascontiguousarray(cw.reshape(CONVT, 128, 4).transpose(1, 0, 2).reshape(128, CONVT * 4)),
        drep=np.ascontiguousarray(np.tile(Dv[None, :], (128, 1))),
        dtb=dtb_c, alog=alog_c,
        or32=np.ones((1, 128), np.float32),
        id32=np.eye(128, dtype=np.float32),
        mask01=(s_idx <= t_idx).astype(np.float32),
        oc16=np.ones((128, 1), dtype=ml_dtypes.bfloat16),
        id16=np.eye(128).astype(ml_dtypes.bfloat16),
        ones32=np.ones((128, 128), np.float32),
    )

    in_maps = []
    for c in range(NCORE):
        hsel = np.zeros((128, 8), np.float32)
        if c > 0:
            hsel[:, c - 1] = 1.0
        pfx8 = np.zeros((8, NHD), np.float32)
        pfx8[:c, :] = 1.0
        selr2 = np.zeros((8, 8), np.float32)
        for j in range(8):
            for k in range(8):
                if j < k < c:
                    selr2[k, j] = 1.0
        m = dict(shared)
        m["xin"] = np.ascontiguousarray(xT[:, c * SL:(c + 1) * SL])
        m["hsel"] = hsel
        m["pfx8"] = pfx8
        m["selr2"] = selr2
        in_maps.append(m)
    return in_maps


def kernel(**inputs):
    nc = build_program()
    in_maps = make_in_maps(inputs)
    res = run_bass_kernel_spmd(nc, in_maps, core_ids=list(range(NCORE)))
    out = np.concatenate([np.asarray(r["out"]).T for r in res.results], axis=0)[None]
    return np.ascontiguousarray(out.astype(np.float32))


# version 09
# speedup vs baseline: 1.0366x; 1.0366x over previous
import numpy as np
from contextlib import ExitStack

import concourse.bass as bass
import concourse.mybir as mybir
from concourse.bass_utils import run_bass_kernel_spmd

F32 = mybir.dt.float32
BF16 = mybir.dt.bfloat16
AF = mybir.ActivationFunctionType
ALU = mybir.AluOpType

NCORE = 8
D = 4096
NT = 32
SL = 1024
NH2 = 512
PROJ_PAD = 8576
MT_IP = 67
CONVT = 34
XBCW = 1028
Q = 128
NCH = 8
NHD = 64
HD = 64
EPS = 1e-5

PHASES = ["p0", "ln1", "p1", "p2a", "p2b", "p2c", "p2d", "p2e", "p2f", "p2h", "p3"]


class Sem:
    def __init__(self, stack, nc, name):
        self.h = stack.enter_context(nc.semaphore(name))
        self.n = 0
        self.name = name

    def inc(self, instr, k=1):
        instr.then_inc(self.h, k)
        self.n += k
        return (self, self.n)

    def tok(self):
        return (self, self.n)


def wait(eng, *tokens):
    for t in tokens:
        if t is None:
            continue
        sem, val = t
        if val > 0:
            eng.wait_ge(sem.h, val)


def build_program(limit="p3", dbg=False, p1_stats=True, p1_sgn=True, p2b_cc=True):
    nc = bass.Bass("TRN2", target_bir_lowering=False)
    limit_i = PHASES.index(limit)

    def on(ph):
        return PHASES.index(ph) <= limit_i

    from concourse.hw_specs import get_activation_tables
    _tab_ids = {}
    for _i, (_nm, _funcs) in enumerate(get_activation_tables(nc.m.arch).items()):
        _tab_ids[_nm] = _i

    _all_sems = []

    def bar_toks():
        return [sem.tok() for sem in _all_sems]

    def load_act(s_eng, set_name):
        s_eng.add_instruction(mybir.InstLoadActFuncSet(
            name=nc.get_next_instruction_name(),
            act_func_set_id=_tab_ids[set_name], ins=[], outs=[]))

    ein = lambda n, s, d_: nc.dram_tensor(n, s, d_, kind="ExternalInput").ap()
    xin = ein("xin", [D, SL], F32)
    wffn = ein("wffn", [D, D], F32)
    wip = ein("wip", [D, PROJ_PAD], F32)
    wop = ein("wop", [D, D], F32)
    lnw_c = ein("lnw_c", [128, NT], F32)
    lnw_rd = ein("lnw_r", [1, D], F32)
    nrmw_c = ein("nrmw_c", [128, NT], F32)
    bias_c = ein("bias_c", [128, NT], F32)
    mup_c = ein("mup_c", [128, MT_IP], F32)
    cw_c = ein("cw_c", [128, CONVT * 4], F32)
    drep_d = ein("drep", [128, NHD], F32)
    dtb_d = ein("dtb", [128, 1], F32)
    alog_d = ein("alog", [128, 1], F32)
    or32_d = ein("or32", [1, 128], F32)
    id32_d = ein("id32", [128, 128], F32)
    mask01_d = ein("mask01", [128, 128], F32)
    oc16_d = ein("oc16", [128, 1], BF16)
    id16_d = ein("id16", [128, 128], BF16)
    ones32_d = ein("ones32", [128, 128], F32)
    hsel_d = ein("hsel", [128, 8], F32)
    pfx8_d = ein("pfx8", [8, NHD], F32)
    selr2_d = ein("selr2", [8, 8], F32)

    out_d = nc.dram_tensor("out", [D, SL], F32, kind="ExternalOutput").ap()

    ikind = "ExternalOutput" if dbg else "Internal"
    XR = nc.dram_tensor("XR", [D, SL], F32, kind=ikind).ap()
    SGN = nc.dram_tensor("SGN", [D, D], BF16, kind="Internal").ap()
    ZSP = nc.dram_tensor("ZSP", [D, SL], BF16, kind=ikind).ap()
    cc1i = nc.dram_tensor("cc1i", [128, CONVT * 3], BF16)
    cc1o = nc.dram_tensor("cc1o", [128 * NCORE, CONVT * 3], BF16)
    cc2hi = nc.dram_tensor("cc2hi", [128, D], BF16)
    cc2ho = nc.dram_tensor("cc2ho", [128 * NCORE, D], BF16)
    cc2li = nc.dram_tensor("cc2li", [1, NHD], F32)
    cc2lo = nc.dram_tensor("cc2lo", [NCORE, NHD], F32)
    dbgt = {}
    if dbg:
        for nm, shp in [("DXC", [D, SL]), ("DXB", [128, CONVT * XBCW]),
                        ("DXS", [128, NCH * D]), ("DBT", [128, SL]), ("DCT", [128, SL]),
                        ("DDT", [128, SL]), ("DLA", [128, SL]), ("DHZ", [128, D]),
                        ("DHI", [128, D]), ("DG", [128, NT * SL]), ("DYN", [128, NT * SL]),
                        ("DX1", [D, SL])]:
            dbgt[nm] = nc.dram_tensor(nm, shp, F32, kind="ExternalOutput").ap()

    st = ExitStack()
    with st:
        dW = Sem(st, nc, "dW")
        dG = Sem(st, nc, "dG")
        dS = Sem(st, nc, "dS")
        pe = Sem(st, nc, "pe")
        ve = Sem(st, nc, "ve")
        se = Sem(st, nc, "se")
        ccs = Sem(st, nc, "ccs")
        fin = Sem(st, nc, "fin")
        _all_sems.extend([dW, dG, dS, pe, ve, se, ccs])
        vch = Sem(st, nc, "vch")
        sch = Sem(st, nc, "sch")
        gch = Sem(st, nc, "gch")
        dCst = Sem(st, nc, "dCst")
        dX = [Sem(st, nc, "dX0"), Sem(st, nc, "dX1")]
        dWs = [Sem(st, nc, "dWs0"), Sem(st, nc, "dWs1")]
        dR = [Sem(st, nc, "dR0"), Sem(st, nc, "dR1")]
        dSt = [Sem(st, nc, "dSt0"), Sem(st, nc, "dSt1")]
        dN = [Sem(st, nc, "dN0"), Sem(st, nc, "dN1")]
        dZ = [Sem(st, nc, "dZ0"), Sem(st, nc, "dZ1")]
        dCC = Sem(st, nc, "dCC")
        dWg = [Sem(st, nc, "dWg0"), Sem(st, nc, "dWg1")]
        dRH = [Sem(st, nc, "dRH0"), Sem(st, nc, "dRH1")]
        _all_sems.extend([dCst, dX[0], dX[1], dWs[0], dWs[1], dR[0], dR[1],
                          dSt[0], dSt[1], dN[0], dN[1], dZ[0], dZ[1], dCC,
                          dWg[0], dWg[1], dRH[0], dRH[1]])

        def VC(v_eng, ins):
            wait(v_eng, vch.inc(ins))
            return ins

        def SC(s_eng, ins):
            wait(s_eng, sch.inc(ins))
            return ins

        def inc_chain(eng, sem, ins):
            tok = sem.inc(ins)
            wait(eng, tok)
            return tok

        SB = lambda n, s, d_, side=None: st.enter_context(
            nc.sbuf_tensor(n, s, d_) if side is None else nc.sbuf_tensor(n, s, d_, side=side))

        c_lnw = SB("c_lnw", [128, NT], F32)
        c_nrmw = SB("c_nrmw", [128, NT], F32)
        c_hb = SB("c_hb", [128, NT], F32)
        c_mup = SB("c_mup", [128, MT_IP], F32)
        c_cw = SB("c_cw", [128, CONVT * 4], F32)
        c_drep = SB("c_drep", [128, NHD], F32)
        c_dtb = SB("c_dtb", [128, 1], F32)
        c_alog = SB("c_alog", [128, 1], F32)
        c_or32 = SB("c_or32", [1, 128], F32)
        c_id32 = SB("c_id32", [128, 128], F32)
        c_mask01 = SB("c_mask01", [128, 128], F32)
        c_oc16 = SB("c_oc16", [128, 1], BF16)
        c_id16 = SB("c_id16", [128, 128], BF16)
        c_ones32 = SB("c_ones32", [128, 128], F32)
        c_hsel = SB("c_hsel", [128, 8], F32)
        c_pfx8 = SB("c_pfx8", [8, NHD], F32)
        c_selr2 = SB("c_selr2", [8, 8], F32)
        c_acol = SB("c_acol", [128, 1], F32)
        c_eps1 = SB("c_eps1", [1, 1], F32)
        c_one1 = SB("c_one1", [128, 1], F32)
        r_m = SB("r_m", [1, SL], F32)
        r_scl = SB("r_scl", [1, SL], F32)
        r_mscl = SB("r_mscl", [1, SL], F32)
        r_t1 = SB("r_t1", [1, SL], F32)
        r_t2 = SB("r_t2", [1, SL], F32)
        BIGA = SB("BIGA", [128, NT * SL], BF16, side="right")
        BIGB = SB("BIGB", [128, CONVT * XBCW], BF16, side="right")

        biga_t = lambda k: BIGA[:, k * SL:(k + 1) * SL]
        bigb_t = lambda k: BIGB[:, k * SL:(k + 1) * SL]

        def strip_view(w, m):
            return w.rearrange("(k p) mj -> p k mj", p=128)[:, :, m * 128:(m + 1) * 128]

        def sgn_strip_view(m):
            return SGN[m * 128:(m + 1) * 128, :].rearrange("p (k j) -> p k j", k=NT)

        with nc.Block() as blk:
            @blk.gpsimd
            def _(g):
                for dst, src in [(c_lnw, lnw_c), (c_nrmw, nrmw_c), (c_hb, bias_c),
                                 (c_mup, mup_c), (c_cw, cw_c), (c_drep, drep_d),
                                 (c_dtb, dtb_d), (c_alog, alog_d), (c_or32, or32_d),
                                 (c_id32, id32_d), (c_mask01, mask01_d), (c_oc16, oc16_d),
                                 (c_id16, id16_d), (c_ones32, ones32_d), (c_hsel, hsel_d),
                                 (c_pfx8, pfx8_d), (c_selr2, selr2_d)]:
                    dCst.inc(g.dma_start(out=dst[:, :], in_=src), 16)

            @blk.vector
            def _(v):
                v.memset(c_eps1[:, :], EPS)
                ve.inc(v.memset(c_one1[:, :], 1.0))

            @blk.scalar
            def _(s):
                load_act(s, "natural_log_exp_and_others")
                wait(s, dCst.tok(), (ve, ve.n))
                s.mul(c_hb[:, :], c_hb[:, :], 0.5)
                SC(s, s.activation(c_acol[:, :], c_alog[:, :], AF.Exp))
                se.inc(s.mul(c_acol[:, :], c_acol[:, :], -1.0))
        setup_tok = se.tok()

        def emit_ln_rows(blk, fold_rms, entry_toks):
            v0, s0 = ve.n, se.n

            @blk.scalar
            def _(s):
                wait(s, (ve, v0 + 2))
                SC(s, s.activation(r_t1[0:1, :], r_t2[0:1, :], AF.Ln))
                se.inc(s.activation(r_scl[0:1, :], r_t1[0:1, :], AF.Exp, scale=-0.5))

            @blk.vector
            def _(v):
                wait(v, *entry_toks)
                VC(v, v.tensor_tensor(r_t2[0:1, :], r_m[0:1, :], r_m[0:1, :], ALU.mult))
                inc_chain(v, ve, v.tensor_tensor(r_t1[0:1, :], r_t1[0:1, :], r_t2[0:1, :], ALU.subtract))
                if not fold_rms:
                    ve.inc(v.tensor_scalar(r_t2[0:1, :], r_t1[0:1, :], float(EPS), None, ALU.add))
                else:
                    VC(v, v.tensor_scalar(r_scl[0:1, :], r_t1[0:1, :], float(EPS), None, ALU.add))
                    VC(v, v.reciprocal(r_t2[0:1, :], r_scl[0:1, :]))
                    VC(v, v.tensor_tensor(r_t2[0:1, :], r_t1[0:1, :], r_t2[0:1, :], ALU.mult))
                    VC(v, v.tensor_scalar(r_t2[0:1, :], r_t2[0:1, :], float(EPS), None, ALU.add))
                    ve.inc(v.tensor_tensor(r_t2[0:1, :], r_t2[0:1, :], r_scl[0:1, :], ALU.mult))
                wait(v, (se, s0 + 1))
                ve.inc(v.tensor_tensor(r_mscl[0:1, :], r_m[0:1, :], r_scl[0:1, :], ALU.mult))
            return ve.tok()

        def emit_bcast_rows(blk, entry_tok, P_x, P_y, row_x, row_y, extra=()):
            @blk.tensor
            def _(t):
                wait(t, entry_tok, *extra)
                t.matmul(P_x[:, 0:NH2], c_or32[0:1, :], row_x[0:1, 0:NH2], start=True, stop=True)
                t.matmul(P_x[:, NH2:SL], c_or32[0:1, :], row_x[0:1, NH2:SL], start=True, stop=True)
                t.matmul(P_y[:, 0:NH2], c_or32[0:1, :], row_y[0:1, 0:NH2], start=True, stop=True)
                pe.inc(t.matmul(P_y[:, NH2:SL], c_or32[0:1, :], row_y[0:1, NH2:SL], start=True, stop=True))
            return pe.tok()

        ctx0 = ExitStack()
        sq0 = ctx0.enter_context(nc.sbuf_tensor("sq0", [128, 4 * SL], BF16))
        psc0 = ExitStack()
        P_sa = psc0.enter_context(nc.psum_tensor("P_sa", [1, NH2], F32))
        P_sb = psc0.enter_context(nc.psum_tensor("P_sb", [1, NH2], F32))
        P_sc = psc0.enter_context(nc.psum_tensor("P_sc", [1, NH2], F32))
        P_sd = psc0.enter_context(nc.psum_tensor("P_sd", [1, NH2], F32))

        se0, pe0 = se.n, pe.n
        xld_tok = {}
        dX4 = [dX[0], dX[1], dWg[0], dWg[1]]
        with nc.Block() as blk:
            @blk.gpsimd
            def _(g):
                for k in range(NT):
                    if k >= 4:
                        wait(g, (se, se0 + (k - 3)))
                    xld_tok[k] = dX4[k % 4].inc(
                        g.dma_start(out=biga_t(k), in_=xin[k * 128:(k + 1) * 128, :]), 16)

            @blk.scalar
            def _(s):
                wait(s, setup_tok)
                for k in range(NT):
                    wait(s, xld_tok[k])
                    if k >= 4:
                        wait(s, (pe, pe0 + (k - 3)))
                    sl = (k % 4) * SL
                    se.inc(s.activation(sq0[:, sl:sl + SL], biga_t(k), AF.Square))

            @blk.tensor
            def _(t):
                for k in range(NT):
                    wait(t, xld_tok[k], (se, se0 + (k + 1)))
                    first, last = (k == 0), (k == NT - 1)
                    sl = (k % 4) * SL
                    t.matmul(P_sa[0:1, :], c_oc16[:, :], biga_t(k)[:, 0:NH2], start=first, stop=last)
                    t.matmul(P_sb[0:1, :], c_oc16[:, :], biga_t(k)[:, NH2:SL], start=first, stop=last)
                    t.matmul(P_sc[0:1, :], c_oc16[:, :], sq0[:, sl:sl + NH2], start=first, stop=last)
                    pe.inc(t.matmul(P_sd[0:1, :], c_oc16[:, :], sq0[:, sl + NH2:sl + SL], start=first, stop=last))

            @blk.vector
            def _(v):
                wait(v, (pe, pe0 + NT))
                v.tensor_scalar(r_m[0:1, 0:NH2], P_sa[0:1, :], 1.0 / D, None, ALU.mult)
                v.tensor_scalar(r_m[0:1, NH2:SL], P_sb[0:1, :], 1.0 / D, None, ALU.mult)
                v.tensor_scalar(r_t1[0:1, 0:NH2], P_sc[0:1, :], 1.0 / D, None, ALU.mult)
                ve.inc(v.tensor_scalar(r_t1[0:1, NH2:SL], P_sd[0:1, :], 1.0 / D, None, ALU.mult))
        x_loaded = [xld_tok[NT - 4], xld_tok[NT - 3], xld_tok[NT - 2], xld_tok[NT - 1]]

        with nc.Block() as blk:
            rows1_tok = emit_ln_rows(blk, False, [ve.tok()])
        psc0.close()
        if dbg:
            with nc.Block() as blk:
                @blk.gpsimd
                def _(g):
                    wait(g, rows1_tok)
                    dG.inc(g.dma_start(out=dbgt["DXC"], in_=BIGA[:, :]), 16)
                    dG.inc(g.dma_start(out=dbgt["DDT"][0:1, :], in_=r_m[0:1, :]), 16)
                    dG.inc(g.dma_start(out=dbgt["DDT"][1:2, :], in_=r_t1[0:1, :]), 16)
                    dG.inc(g.dma_start(out=dbgt["DDT"][2:3, :], in_=r_scl[0:1, :]), 16)
                    dG.inc(g.dma_start(out=dbgt["DDT"][3:4, :], in_=r_mscl[0:1, :]), 16)
                    dG.inc(g.dma_start(out=dbgt["DXB"][:, 0:2 * SL], in_=sq0[:, :]), 16)

        ln1_tok = None
        pscB = ExitStack()
        P_ba = pscB.enter_context(nc.psum_tensor("P_ba", [128, SL], F32))
        P_bb = pscB.enter_context(nc.psum_tensor("P_bb", [128, SL], F32))
        tmpf0 = ctx0.enter_context(nc.sbuf_tensor("tmpf0", [128, SL], F32))
        SCLB = ctx0.enter_context(nc.sbuf_tensor("sclb", [128, SL], F32))
        MSCB = ctx0.enter_context(nc.sbuf_tensor("mscb", [128, SL], F32))
        if on("ln1"):
            with nc.Block() as blk:
                bc_tok = emit_bcast_rows(blk, rows1_tok, P_ba, P_bb, r_scl, r_mscl)

                @blk.vector
                def _(v):
                    wait(v, bc_tok, *x_loaded)
                    v.tensor_copy(SCLB[:, :], P_ba[:, :])
                    VC(v, v.tensor_copy(MSCB[:, :], P_bb[:, :]))
                    for k in range(NT):
                        VC(v, v.tensor_tensor(tmpf0[:, :], biga_t(k), SCLB[:, :], ALU.mult))
                        inc_chain(v, ve, v.tensor_tensor(bigb_t(k), tmpf0[:, :], MSCB[:, :], ALU.subtract))
            ln1_tok = ve.tok()
        pscB.close()
        ctx0.close()

        _ffn_seq = [0]

        def ffn_pass(blk, opnd_get, opnd_tok, w_src, w_mode, resid_src, dst_dram,
                     do_stats, half, use_bias):
            _ffn_seq[0] += 1
            _sfx = str(_ffn_seq[0])
            """w_mode: 'sign' (fp32 half-strips -> Sign -> bf16, store to SGN),
                       'sgn16' (bf16 strips from SGN via sync),
                       'cast' (fp32 -> bf16 DMA-cast via gpsimd).
            Evac: X1 = half*psum + resid (+ c_hb col). Stores X1 to dst_dram via
            scalar queue. Optionally accumulates mean/E[x^2] into r_m/r_t1."""
            ctx = ExitStack()
            psx = ExitStack()
            HD2 = D // 2
            if w_mode == "sign":
                WF = ctx.enter_context(nc.sbuf_tensor("wf" + _sfx, [128, HD2], F32))
            WST = ctx.enter_context(nc.sbuf_tensor("wst" + _sfx, [128, 2 * D], BF16))
            XT = ctx.enter_context(nc.sbuf_tensor("xt" + _sfx, [128, 2 * SL], F32))
            X1 = ctx.enter_context(nc.sbuf_tensor("x1" + _sfx, [128, 2 * SL], F32))
            if do_stats:
                XB = ctx.enter_context(nc.sbuf_tensor("xb" + _sfx, [128, SL], BF16))
                SQ = ctx.enter_context(nc.sbuf_tensor("sq" + _sfx, [128, SL], BF16))
            P_ma = psx.enter_context(nc.psum_tensor("pma" + _sfx, [128, SL], F32))
            P_mb = psx.enter_context(nc.psum_tensor("pmb" + _sfx, [128, SL], F32))
            if do_stats:
                Psa = psx.enter_context(nc.psum_tensor("psa" + _sfx, [1, NH2], F32))
                Psb = psx.enter_context(nc.psum_tensor("psb" + _sfx, [1, NH2], F32))
                Psc = psx.enter_context(nc.psum_tensor("psc" + _sfx, [1, NH2], F32))
                Psd = psx.enter_context(nc.psum_tensor("psd" + _sfx, [1, NH2], F32))

            pe0, ve0, se0 = pe.n, ve.n, se.n
            BARF = bar_toks()
            SIGN = (w_mode == "sign")
            wdma_tok = {}
            xt_dma_tok = {}
            st_tok = {}
            sg_tok = {}

            VPM = 1 + (1 if use_bias else 0) + (1 if do_stats else 0)

            def v1_tok(m):
                return (ve, ve0 + VPM * m + 1)

            def vlast_tok(m):
                return (ve, ve0 + VPM * (m + 1))

            def pe_mm(m):
                if not do_stats:
                    return (pe, pe0 + m + 1)
                return (pe, pe0 + (1 if m == 0 else 2 * m))

            def pe_st(j):
                return (pe, pe0 + (2 * NT if j == NT - 1 else 2 * j + 3))

            SGI = 2 if SIGN else 0

            def sign_b_tok(m):
                if do_stats:
                    return (se, se0 + (2 if m == 0 else (SGI + 1) * m + SGI - 1))
                return (se, se0 + SGI * (m + 1))

            def sign_a_tok(m):
                sem, v = sign_b_tok(m)
                return (sem, v - 1)

            def sq_tok(j):
                if SIGN:
                    return (se, se0 + ((SGI + 1) * j + SGI + 3 if j < NT - 1
                                       else (SGI + 1) * (NT - 1) + SGI + 1))
                return (se, se0 + j + 1)

            def x1store_tok(j):
                return st_tok[j]

            if w_mode in ("sign", "sgn16"):
                @blk.sync
                def _(sy):
                    wait(sy, *BARF)
                    for m in range(NT):
                        if SIGN:
                            if m >= 1:
                                wait(sy, sign_b_tok(m - 1))
                            wdma_tok[(m, 0)] = dWs[0].inc(sy.dma_start(
                                out=WF[:, :].rearrange("p (k j) -> p k j", k=NT // 2),
                                in_=w_src(m)[:, 0:NT // 2, :]), 16)
                            wait(sy, sign_a_tok(m))
                            wdma_tok[(m, 1)] = dWs[1].inc(sy.dma_start(
                                out=WF[:, :].rearrange("p (k j) -> p k j", k=NT // 2),
                                in_=w_src(m)[:, NT // 2:NT, :]), 16)
                        else:
                            if m >= 2:
                                wait(sy, pe_mm(m - 2))
                            wdma_tok[m] = dWs[m % 2].inc(sy.dma_start(
                                out=WST[:, (m % 2) * D:(m % 2 + 1) * D].rearrange(
                                    "p (k j) -> p k j", k=NT),
                                in_=w_src(m)), 16)

            @blk.gpsimd
            def _(g):
                wait(g, *BARF)
                for m in range(NT):
                    if w_mode == "cast":
                        if m >= 2:
                            wait(g, pe_mm(m - 2))
                        wdma_tok[m] = dWg[m % 2].inc(g.dma_start(
                            out=WST[:, (m % 2) * D:(m % 2 + 1) * D].rearrange(
                                "p (k j) -> p k j", k=NT),
                            in_=w_src(m)), 16)
                    if m >= 2:
                        wait(g, v1_tok(m - 2))
                    xt_dma_tok[m] = dR[m % 2].inc(
                        g.dma_start(out=XT[:, (m % 2) * SL:(m % 2 + 1) * SL],
                                    in_=resid_src[m * 128:(m + 1) * 128, :]), 16)

            def wcast_tok(m):
                return wdma_tok[m]

            def xt_tok(m):
                return xt_dma_tok[m]

            @blk.scalar
            def _(s):
                wait(s, *BARF)
                for m in range(NT):
                    if SIGN:
                        wsl = (m % 2) * D
                        wait(s, wdma_tok[(m, 0)])
                        if m >= 2:
                            wait(s, pe_mm(m - 2), sg_tok[m - 2])
                        inc_chain(s, se, s.activation(WST[:, wsl:wsl + HD2], WF[:, :], AF.Sign))
                        wait(s, wdma_tok[(m, 1)])
                        inc_chain(s, se, s.activation(WST[:, wsl + HD2:wsl + D], WF[:, :], AF.Sign))
                        sg_tok[m] = dN[m % 2].inc(
                            s.dma_start(out=SGN[m * 128:(m + 1) * 128, :],
                                        in_=WST[:, wsl:wsl + D]), 16)
                    if m >= 1:
                        j = m - 1
                        wait(s, vlast_tok(j))
                        if do_stats:
                            if j >= 1:
                                wait(s, pe_st(j - 1))
                            se.inc(s.activation(SQ[:, :], X1[:, (j % 2) * SL:(j % 2 + 1) * SL],
                                                AF.Square))
                        st_tok[j] = dSt[j % 2].inc(
                            s.dma_start(out=dst_dram[j * 128:(j + 1) * 128, :],
                                        in_=X1[:, (j % 2) * SL:(j % 2 + 1) * SL]), 16)
                j = NT - 1
                wait(s, vlast_tok(j))
                if do_stats:
                    wait(s, pe_st(j - 1))
                    se.inc(s.activation(SQ[:, :], X1[:, (j % 2) * SL:(j % 2 + 1) * SL], AF.Square))
                st_tok[j] = dSt[j % 2].inc(
                    s.dma_start(out=dst_dram[j * 128:(j + 1) * 128, :],
                                in_=X1[:, (j % 2) * SL:(j % 2 + 1) * SL]), 16)

            @blk.tensor
            def _(t):
                wait(t, *BARF, opnd_tok)
                for m in range(NT):
                    if SIGN:
                        wait(t, sign_b_tok(m))
                    else:
                        wait(t, wcast_tok(m))
                    if m >= 2:
                        wait(t, v1_tok(m - 2))
                    ps = P_ma if m % 2 == 0 else P_mb
                    wsl = (m % 2) * D
                    last = None
                    for h in range(2):
                        for k in range(NT):
                            last = t.matmul(ps[:, h * NH2:(h + 1) * NH2],
                                            WST[:, wsl + k * 128:wsl + (k + 1) * 128],
                                            opnd_get(k)[:, h * NH2:(h + 1) * NH2],
                                            start=(k == 0), stop=(k == NT - 1))
                    pe.inc(last)
                    if do_stats and m >= 1:
                        j = m - 1
                        wait(t, sq_tok(j), vlast_tok(j))
                        fr = (j == 0)
                        t.matmul(Psa[0:1, :], c_oc16[:, :], XB[:, 0:NH2], start=fr, stop=False)
                        t.matmul(Psb[0:1, :], c_oc16[:, :], XB[:, NH2:SL], start=fr, stop=False)
                        t.matmul(Psc[0:1, :], c_oc16[:, :], SQ[:, 0:NH2], start=fr, stop=False)
                        pe.inc(t.matmul(Psd[0:1, :], c_oc16[:, :], SQ[:, NH2:SL],
                                        start=fr, stop=False))
                if do_stats:
                    j = NT - 1
                    wait(t, sq_tok(j), vlast_tok(j))
                    t.matmul(Psa[0:1, :], c_oc16[:, :], XB[:, 0:NH2], start=False, stop=True)
                    t.matmul(Psb[0:1, :], c_oc16[:, :], XB[:, NH2:SL], start=False, stop=True)
                    t.matmul(Psc[0:1, :], c_oc16[:, :], SQ[:, 0:NH2], start=False, stop=True)
                    pe.inc(t.matmul(Psd[0:1, :], c_oc16[:, :], SQ[:, NH2:SL],
                                    start=False, stop=True))

            @blk.vector
            def _(v):
                wait(v, *BARF)
                for m in range(NT):
                    wait(v, pe_mm(m), xt_tok(m))
                    if m >= 2:
                        wait(v, x1store_tok(m - 2))
                        if do_stats:
                            wait(v, sq_tok(m - 2))
                    sl = (m % 2) * SL
                    inc_chain(v, ve, v.scalar_tensor_tensor(X1[:, sl:sl + SL],
                                             (P_ma if m % 2 == 0 else P_mb)[:, :],
                                             float(half), XT[:, sl:sl + SL],
                                             ALU.mult, ALU.add))
                    if use_bias:
                        inc_chain(v, ve, v.tensor_scalar(X1[:, sl:sl + SL], X1[:, sl:sl + SL],
                                      c_hb[:, m:m + 1], None, ALU.add))
                    if do_stats:
                        if m >= 1:
                            wait(v, pe_st(m - 1))
                        ve.inc(v.tensor_copy(XB[:, :], X1[:, sl:sl + SL]))

            ret_tok = None
            if do_stats:
                @blk.vector
                def _(v):
                    wait(v, pe_st(NT - 1))
                    v.tensor_scalar(r_m[0:1, 0:NH2], Psa[0:1, :], 1.0 / D, None, ALU.mult)
                    v.tensor_scalar(r_m[0:1, NH2:SL], Psb[0:1, :], 1.0 / D, None, ALU.mult)
                    v.tensor_scalar(r_t1[0:1, 0:NH2], Psc[0:1, :], 1.0 / D, None, ALU.mult)
                    ve.inc(v.tensor_scalar(r_t1[0:1, NH2:SL], Psd[0:1, :], 1.0 / D, None, ALU.mult))
                ret_tok = ve.tok()
            return ctx, psx, ret_tok, [st_tok[j] for j in range(NT)]

        if on("p1"):
            with nc.Block() as blk:
                ctx1, psx1, stats2_raw, x1st_tok_by_tile = ffn_pass(
                    blk, bigb_t, ln1_tok, lambda m: strip_view(wffn, m),
                    "sign" if p1_sgn else "cast",
                    xin, XR, p1_stats, 0.5, True)
            psx1.close()
            ctx1.close()

        if on("p2a"):
            with nc.Block() as blk:
                rows2_tok = emit_ln_rows(blk, True, [stats2_raw])

            ctx2a = ExitStack()
            psx2a = ExitStack()
            x1t = ctx2a.enter_context(nc.sbuf_tensor("x1t", [128, 2 * SL], F32))
            tmpa = ctx2a.enter_context(nc.sbuf_tensor("tmpa", [128, SL], F32))
            Pba2 = psx2a.enter_context(nc.psum_tensor("Pba2", [128, SL], F32))
            Pbb2 = psx2a.enter_context(nc.psum_tensor("Pbb2", [128, SL], F32))
            pe0, ve0 = pe.n, ve.n
            BAR2A = bar_toks()
            x1t_tok = {}
            with nc.Block() as blk:
                @blk.sync
                def _(sy):
                    wait(sy, *BAR2A)
                    for k in range(NT):
                        if k >= 2:
                            wait(sy, (ve, ve0 + 3 * (k - 2) + 3))
                        wait(sy, x1st_tok_by_tile[k])
                        x1t_tok[k] = dRH[k % 2].inc(
                            sy.dma_start(out=x1t[:, (k % 2) * SL:(k % 2 + 1) * SL],
                                         in_=XR[k * 128:(k + 1) * 128, :]), 16)

                bc2 = emit_bcast_rows(blk, rows2_tok, Pba2, Pbb2, r_scl, r_mscl,
                                      extra=BAR2A)

                SCB2 = ctx2a.enter_context(nc.sbuf_tensor("scb2", [128, SL], F32))
                MSB2 = ctx2a.enter_context(nc.sbuf_tensor("msb2", [128, SL], F32))

                @blk.vector
                def _(v):
                    wait(v, *BAR2A, bc2)
                    v.tensor_copy(SCB2[:, :], Pba2[:, :])
                    VC(v, v.tensor_copy(MSB2[:, :], Pbb2[:, :]))
                    for k in range(NT):
                        wait(v, x1t_tok[k])
                        sl = (k % 2) * SL
                        inc_chain(v, ve, v.tensor_tensor(tmpa[:, :], x1t[:, sl:sl + SL], SCB2[:, :], ALU.mult))
                        inc_chain(v, ve, v.tensor_tensor(tmpa[:, :], tmpa[:, :], MSB2[:, :], ALU.subtract))
                        inc_chain(v, ve, v.tensor_scalar(biga_t(k), tmpa[:, :], c_lnw[:, k:k + 1], None, ALU.mult))
            xc_tok = ve.tok()
            psx2a.close()
            ctx2a.close()
            if dbg:
                with nc.Block() as blk:
                    @blk.gpsimd
                    def _(g):
                        wait(g, xc_tok)
                        for k in range(NT):
                            dG.inc(g.dma_start(out=dbgt["DXC"][k * 128:(k + 1) * 128, :], in_=biga_t(k)), 16)
                        wait(g, (dS, dS.n))
                        for k in range(NT):
                            dG.inc(g.dma_start(out=dbgt["DX1"][k * 128:(k + 1) * 128, :],
                                               in_=XR[k * 128:(k + 1) * 128, :]), 16)

        if on("p2b"):
            ctx2b = ExitStack()
            psx2b = ExitStack()
            ctxDTR = ExitStack()
            DTR = ctxDTR.enter_context(nc.sbuf_tensor("dtr", [128, SL], F32, side="right"))
            IPS = ctx2b.enter_context(nc.sbuf_tensor("ips", [128, 2 * D], BF16))
            ZT = ctx2b.enter_context(nc.sbuf_tensor("zt", [128, 2 * SL], BF16))
            Pia = psx2b.enter_context(nc.psum_tensor("pia", [128, SL], F32))
            Pib = psx2b.enter_context(nc.psum_tensor("pib", [128, SL], F32))
            pe0, ve0, se0 = pe.n, ve.n, se.n
            ipst_tok = {}
            zst_tok = {}
            evac_tok = []

            def ip_evac_plan(m):
                return (ve, ve0 + m + 1)

            SGZ = ctx2b.enter_context(nc.sbuf_tensor("sgz", [128, 2 * SL], BF16))
            BAR2B = bar_toks()
            with nc.Block() as blk:
                @blk.scalar
                def _(s):
                    wait(s, *BAR2B)
                    load_act(s, "sigmoid_and_others")
                    for m in range(NT):
                        wait(s, (pe, pe0 + m + 1))
                        if m >= 2:
                            wait(s, (ve, ve0 + m - 1))
                        sl = (m % 2) * SL
                        se.inc(s.activation(SGZ[:, sl:sl + SL], (Pia if m % 2 == 0 else Pib)[:, :],
                                            AF.Sigmoid, scale=c_mup[:, m:m + 1]))
                        if m >= 1:
                            wait(s, (ve, ve0 + m))
                            zst_tok[m - 1] = dZ[(m - 1) % 2].inc(
                                s.dma_start(out=ZSP[(m - 1) * 128:m * 128, :],
                                            in_=ZT[:, ((m - 1) % 2) * SL:((m - 1) % 2 + 1) * SL]), 16)

                @blk.scalar
                def _(s):
                    wait(s, (ve, ve0 + NT))
                    zst_tok[NT - 1] = dZ[(NT - 1) % 2].inc(
                        s.dma_start(out=ZSP[(NT - 1) * 128:NT * 128, :],
                                    in_=ZT[:, ((NT - 1) % 2) * SL:((NT - 1) % 2 + 1) * SL]), 16)

                @blk.gpsimd
                def _(g):
                    wait(g, *BAR2B)
                    for m in range(MT_IP):
                        if m >= 2:
                            wait(g, (pe, pe0 + m - 1))
                        ipst_tok[m] = dWg[m % 2].inc(g.dma_start(
                            out=IPS[:, (m % 2) * D:(m % 2 + 1) * D].rearrange("p (k j) -> p k j", k=NT),
                            in_=strip_view(wip, m)), 16)

                @blk.tensor
                def _(t):
                    wait(t, *BAR2B, xc_tok)
                    for m in range(MT_IP):
                        wait(t, ipst_tok[m])
                        if m >= 2:
                            wait(t, ip_evac_plan(m - 2))
                        ps = Pia if m % 2 == 0 else Pib
                        wsl = (m % 2) * D
                        last = None
                        for h in range(2):
                            for k in range(NT):
                                last = t.matmul(ps[:, h * NH2:(h + 1) * NH2],
                                                IPS[:, wsl + k * 128:wsl + (k + 1) * 128],
                                                biga_t(k)[:, h * NH2:(h + 1) * NH2],
                                                start=(k == 0), stop=(k == NT - 1))
                        pe.inc(last)

                @blk.vector
                def _(v):
                    wait(v, *BAR2B)
                    for ct in range(CONVT):
                        ins = v.memset(BIGB[:, ct * XBCW:ct * XBCW + 4], 0)
                    VC(v, ins)
                    for m in range(MT_IP):
                        wait(v, (pe, pe0 + m + 1))
                        ps = (Pia if m % 2 == 0 else Pib)
                        if m < NT:
                            sl = (m % 2) * SL
                            wait(v, (se, se0 + m + 1))
                            if m >= 2:
                                wait(v, zst_tok[m - 2])
                            ve.inc(v.scalar_tensor_tensor(ZT[:, sl:sl + SL], ps[:, :],
                                                          c_mup[:, m:m + 1], SGZ[:, sl:sl + SL],
                                                          ALU.mult, ALU.mult))
                        elif m < NT + CONVT:
                            ct = m - NT
                            ve.inc(v.tensor_scalar(BIGB[:, ct * XBCW + 4:ct * XBCW + 4 + SL],
                                                   ps[:, :], c_mup[:, m:m + 1], None, ALU.mult))
                        else:
                            ve.inc(v.tensor_scalar(DTR[:, :], ps[:, :], c_mup[:, m:m + 1], None, ALU.mult))

            xbc_tok = (ve, ve0 + NT + CONVT)
            dt_tok = (ve, ve0 + MT_IP)
            if not p2b_cc:
                with nc.Block() as blk:
                    @blk.vector
                    def _(v):
                        wait(v, xbc_tok)
                        ve.inc(v.engine_nop())
                halo_tok = ve.tok()
            ctxcc1 = ExitStack()
            HSLOT = ctxcc1.enter_context(nc.sbuf_tensor("hslot", [128, 8 * CONVT * 3], BF16))
            HALO = ctxcc1.enter_context(nc.sbuf_tensor("halo", [128, CONVT * 3], F32))
            TAILS = ctxcc1.enter_context(nc.sbuf_tensor("tails", [128, CONVT * 3], BF16))
            ve1 = ve.n
            with nc.Block() as blk:
                @blk.vector
                def _(v):
                    if not p2b_cc:
                        return
                    wait(v, xbc_tok)
                    ins = None
                    for ct in range(CONVT):
                        ins = v.tensor_copy(TAILS[:, ct * 3:(ct + 1) * 3],
                                            BIGB[:, (ct + 1) * XBCW - 3:(ct + 1) * XBCW])
                    ve.inc(ins)

                @blk.gpsimd
                def _(g):
                    if not p2b_cc:
                        return
                    wait(g, (ve, ve1 + 1))
                    dCC.inc(g.dma_start(out=cc1i.ap(), in_=TAILS[:, :]), 16)
                    wait(g, dCC.tok())
                    ccs.inc(g.collective_compute(
                        "AllGather", ALU.bypass, replica_groups=[list(range(NCORE))],
                        ins=[cc1i.ap()], outs=[cc1o.ap()]), 1)
                    wait(g, (ccs, 1))
                    for j in range(NCORE):
                        dCC.inc(g.dma_start(out=HSLOT[:, j * CONVT * 3:(j + 1) * CONVT * 3],
                                            in_=cc1o.ap()[j * 128:(j + 1) * 128, :]), 16)

                @blk.vector
                def _(v):
                    if not p2b_cc:
                        return
                    wait(v, dCC.tok())
                    VC(v, v.tensor_scalar(HALO[:, :], HSLOT[:, 0:CONVT * 3], c_hsel[:, 0:1], None, ALU.mult))
                    for j in range(1, NCORE):
                        VC(v, v.scalar_tensor_tensor(HALO[:, :], HSLOT[:, j * CONVT * 3:(j + 1) * CONVT * 3],
                                                     c_hsel[:, j:j + 1], HALO[:, :], ALU.mult, ALU.add))
                    for ct in range(CONVT):
                        ins = v.tensor_copy(BIGB[:, ct * XBCW + 1:ct * XBCW + 4],
                                            HALO[:, ct * 3:(ct + 1) * 3])
                        ve.inc(ins)
            if p2b_cc:
                halo_tok = ve.tok()
            ctxcc1.close()
            psx2b.close()
            ctx2b.close()
            if dbg:
                with nc.Block() as blk:
                    @blk.gpsimd
                    def _(g):
                        wait(g, halo_tok)
                        dG.inc(g.dma_start(out=dbgt["DXB"], in_=BIGB[:, :]), 16)

        xsS = lambda c: BIGA[:, c * D:(c + 1) * D]
        if on("p2c"):
            ctxBCB = ExitStack()
            BT16 = ctxBCB.enter_context(nc.sbuf_tensor("bt16", [128, SL], BF16, side="right"))
            CT16 = ctxBCB.enter_context(nc.sbuf_tensor("ct16", [128, SL], BF16, side="right"))
            BS16 = ctxBCB.enter_context(nc.sbuf_tensor("bs16", [128, SL], BF16, side="right"))
            ctx2c = ExitStack()
            psx2c = ExitStack()
            ACC = ctx2c.enter_context(nc.sbuf_tensor("acc", [128, 2 * SL], F32))
            SGT = ctx2c.enter_context(nc.sbuf_tensor("sgt", [128, 2 * SL], BF16))
            XTMP = ctx2c.enter_context(nc.sbuf_tensor("xtmp", [128, 2 * SL], BF16))
            PT0 = psx2c.enter_context(nc.psum_tensor("pt0", [128, SL], BF16))
            PT1 = psx2c.enter_context(nc.psum_tensor("pt1", [128, SL], BF16))
            pe0, ve0, se0 = pe.n, ve.n, se.n
            pe0, ve0, se0 = pe.n, ve.n, se.n
            BAR2C = bar_toks()

            def s_sg(ct):
                return (se, se0 + ct + 1)

            def pe_tr(ct):
                return (pe, pe0 + ct + 1)

            conv_tok = {}
            mult_tok = {}
            evac_tok = {}
            bs_tok = [None]

            with nc.Block() as blk:
                @blk.vector
                def _(v):
                    wait(v, *BAR2C)
                    wait(v, halo_tok)

                    def do_mult(j):
                        wait(v, s_sg(j))
                        jsl = (j % 2) * SL
                        if j < NT:
                            dst = XTMP[:, jsl:jsl + SL]
                        elif j == NT:
                            dst = BT16[:, :]
                        else:
                            dst = CT16[:, :]
                        mult_tok[j] = ve.inc(v.tensor_tensor(dst, ACC[:, jsl:jsl + SL],
                                                             SGT[:, jsl:jsl + SL], ALU.mult))

                    def do_evac(j):
                        wait(v, pe_tr(j))
                        ins = None
                        for c in range(NCH):
                            ins = v.tensor_copy(xsS(c)[:, j * 128:(j + 1) * 128],
                                                (PT0 if j % 2 == 0 else PT1)[:, c * 128:(c + 1) * 128])
                        evac_tok[j] = ve.inc(ins)

                    for ct in range(CONVT):
                        if ct >= 2:
                            wait(v, s_sg(ct - 1))
                            wait(v, mult_tok[ct - 2])
                        sl = (ct % 2) * SL
                        base = ct * XBCW
                        VC(v, v.tensor_scalar(ACC[:, sl:sl + SL], BIGB[:, base + 1:base + 1 + SL],
                                              c_cw[:, ct * 4:ct * 4 + 1], None, ALU.mult))
                        for kk in range(1, 4):
                            op = v.scalar_tensor_tensor(ACC[:, sl:sl + SL],
                                                        BIGB[:, base + 1 + kk:base + 1 + kk + SL],
                                                        c_cw[:, ct * 4 + kk:ct * 4 + kk + 1],
                                                        ACC[:, sl:sl + SL], ALU.mult, ALU.add)
                            if kk < 3:
                                VC(v, op)
                            else:
                                conv_tok[ct] = ve.inc(op)
                                wait(v, conv_tok[ct])
                        if ct >= 1:
                            if ct - 1 >= 2 and ct - 1 < NT:
                                wait(v, pe_tr(ct - 3))
                            do_mult(ct - 1)
                        if ct >= 3 and ct - 3 < NT:
                            do_evac(ct - 3)
                    do_mult(CONVT - 1)
                    for j in range(CONVT - 3, NT):
                        do_evac(j)
                    wait(v, (pe, pe0 + NT + 1))
                    ins = None
                    for c in range(NCH):
                        ins = v.tensor_copy(BS16[:, c * 128:(c + 1) * 128], PT0[:, c * 128:(c + 1) * 128])
                    bs_tok[0] = ve.inc(ins)

                @blk.scalar
                def _(s):
                    wait(s, *BAR2C)
                    for ct in range(CONVT):
                        wait(s, conv_tok[ct])
                        if ct >= 2:
                            wait(s, mult_tok[ct - 2])
                        sl = (ct % 2) * SL
                        se.inc(s.activation(SGT[:, sl:sl + SL], ACC[:, sl:sl + SL], AF.Sigmoid))

                @blk.tensor
                def _(t):
                    wait(t, *BAR2C)
                    for ct in range(NT):
                        wait(t, mult_tok[ct])
                        if ct >= 2:
                            wait(t, evac_tok[ct - 2])
                        sl = (ct % 2) * SL
                        pt = PT0 if ct % 2 == 0 else PT1
                        last = None
                        for c in range(NCH):
                            last = t.transpose(pt[:, c * 128:(c + 1) * 128],
                                               XTMP[:, sl + c * 128:sl + (c + 1) * 128], c_id16[:, :])
                        pe.inc(last)
                    wait(t, mult_tok[NT], evac_tok[NT - 2])
                    last = None
                    for c in range(NCH):
                        last = t.transpose(PT0[:, c * 128:(c + 1) * 128],
                                           BT16[:, c * 128:(c + 1) * 128], c_id16[:, :])
                    pe.inc(last)
            xs_tok = bs_tok[0]
            ct_tok = mult_tok[CONVT - 1]
            psx2c.close()
            ctx2c.close()
            if dbg:
                with nc.Block() as blk:
                    @blk.gpsimd
                    def _(g):
                        wait(g, xs_tok, ct_tok)
                        dG.inc(g.dma_start(out=dbgt["DXS"], in_=BIGA[:, 0:NCH * D]), 16)
                        dG.inc(g.dma_start(out=dbgt["DBT"], in_=BT16[:, :]), 16)
                        dG.inc(g.dma_start(out=dbgt["DCT"], in_=CT16[:, :]), 16)


        if on("p2d"):
            ctxSSD = ExitStack()
            C_bias = ctxSSD.enter_context(nc.sbuf_tensor("c_biasS", [128, NCH * 64], F32))
            C_wscl = ctxSSD.enter_context(nc.sbuf_tensor("c_wsclS", [128, NCH * 64], F32))
            C_dec = ctxSSD.enter_context(nc.sbuf_tensor("c_decS", [128, NCH * 64], F32))
            C_laS = ctxSSD.enter_context(nc.sbuf_tensor("c_laS", [128, NCH * 64], F32))
            HRUN = ctxSSD.enter_context(nc.sbuf_tensor("hrun", [128, D], BF16))
            LSEG = ctxSSD.enter_context(nc.sbuf_tensor("lseg", [64, 1], F32))

            ctx2d = ExitStack()
            psx2d = ExitStack()
            DTF = ctx2d.enter_context(nc.sbuf_tensor("dtf", [64, SL], F32))
            LNDT = ctx2d.enter_context(nc.sbuf_tensor("lndt", [64, SL], F32))
            LAH = ctx2d.enter_context(nc.sbuf_tensor("lah", [64, SL], F32))
            T4 = ctx2d.enter_context(nc.sbuf_tensor("t4_", [64, SL], F32))
            BIASH = ctx2d.enter_context(nc.sbuf_tensor("biash", [64, SL], F32))
            WSH = ctx2d.enter_context(nc.sbuf_tensor("wsh", [64, SL], F32))
            LAE = ctx2d.enter_context(nc.sbuf_tensor("lae", [64, 8], F32))
            DECH = T4
            PTD = psx2d.enter_context(nc.psum_tensor("ptd", [128, 1024], F32))
            BAR2D = bar_toks()
            pe0, ve0, se0 = pe.n, ve.n, se.n
            tokd = {}
            with nc.Block() as blk:
                @blk.scalar
                def _(s):
                    wait(s, *BAR2D)
                    load_act(s, "natural_log_exp_and_others")
                    SC(s, s.activation(LNDT[:, :], DTR[0:64, :], AF.Abs, bias=c_dtb[0:64, :]))
                    SC(s, s.activation(LNDT[:, :], LNDT[:, :], AF.Exp, scale=-1.0))
                    inc_chain(s, se, s.activation(LNDT[:, :], LNDT[:, :], AF.Ln, bias=c_one1[0:64, :]))
                    wait(s, (ve, ve0 + 1))
                    inc_chain(s, se, s.activation(LNDT[:, :], DTF[:, :], AF.Ln))
                    wait(s, (ve, ve0 + 3))
                    inc_chain(s, se, s.activation(WSH[:, :], WSH[:, :], AF.Exp, scale=-1.0))
                    wait(s, (ve, ve0 + 4))
                    inc_chain(s, se, s.activation(LAE[:, :], LAE[:, :], AF.Exp))

                @blk.vector
                def _(v):
                    wait(v, *BAR2D)
                    VC(v, v.tensor_scalar(T4[:, :], DTR[0:64, :], c_dtb[0:64, :], 0.0,
                                          ALU.add, ALU.max))
                    wait(v, (se, se0 + 1))
                    inc_chain(v, ve, v.tensor_tensor(DTF[:, :], T4[:, :], LNDT[:, :], ALU.add))
                    VC(v, v.tensor_scalar(T4[:, :], DTF[:, :], c_acol[0:64, :], None, ALU.mult))
                    for c in range(NCH):
                        VC(v, v.tensor_tensor_scan(LAH[:, c * Q:(c + 1) * Q], T4[:, c * Q:(c + 1) * Q],
                                                   T4[:, c * Q:(c + 1) * Q], 0.0, ALU.add, ALU.bypass))
                    VC(v, v.tensor_copy(LSEG[:, :], LAH[:, Q - 1:Q]))
                    for c in range(1, NCH):
                        VC(v, v.tensor_tensor(LSEG[:, :], LSEG[:, :],
                                              LAH[:, (c + 1) * Q - 1:(c + 1) * Q], ALU.add))
                    wait(v, (se, se0 + 2))
                    inc_chain(v, ve, v.tensor_tensor(BIASH[:, :], LNDT[:, :], LAH[:, :], ALU.subtract))
                    for c in range(NCH):
                        VC(v, v.tensor_scalar(WSH[:, c * Q:(c + 1) * Q], LAH[:, c * Q:(c + 1) * Q],
                                              LAH[:, (c + 1) * Q - 1:(c + 1) * Q], None, ALU.subtract))
                    inc_chain(v, ve, v.tensor_copy(LAE[:, 0:1], LAH[:, Q - 1:Q]))
                    for c in range(1, NCH):
                        VC(v, v.tensor_copy(LAE[:, c:c + 1], LAH[:, (c + 1) * Q - 1:(c + 1) * Q]))
                    inc_chain(v, ve, v.tensor_copy(LAE[:, 0:1], LAH[:, Q - 1:Q]))
                    wait(v, (se, se0 + 3))
                    inc_chain(v, ve, v.tensor_tensor(WSH[:, :], WSH[:, :], DTF[:, :], ALU.mult))
                    wait(v, (se, se0 + 4))
                    for c in range(NCH):
                        VC(v, v.tensor_scalar(DECH[:, c * Q:(c + 1) * Q], c_ones32[0:64, 0:Q],
                                              LAE[:, c:c + 1], None, ALU.mult))
                    inc_chain(v, ve, v.tensor_copy(LAE[:, 0:1], LAE[:, 0:1]))
                    for c in range(NCH):
                        wait(v, (pe, pe0 + c + 1))
                        ph = (c % 2) * 512
                        v.tensor_copy(C_bias[:, c * 64:(c + 1) * 64], PTD[:, ph + 0:ph + 64])
                        v.tensor_copy(C_wscl[:, c * 64:(c + 1) * 64], PTD[:, ph + 64:ph + 128])
                        v.tensor_copy(C_dec[:, c * 64:(c + 1) * 64], PTD[:, ph + 128:ph + 192])
                        ve.inc(v.tensor_copy(C_laS[:, c * 64:(c + 1) * 64], PTD[:, ph + 192:ph + 256]))

                @blk.tensor
                def _(t):
                    wait(t, *BAR2D)
                    wait(t, (ve, ve0 + 6))
                    for c in range(NCH):
                        ph = (c % 2) * 512
                        if c >= 2:
                            wait(t, (ve, ve0 + 6 + (c - 2) + 1))
                        t.transpose(PTD[:, ph + 0:ph + 64], BIASH[0:64, c * Q:(c + 1) * Q], c_id32[0:64, 0:64])
                        t.transpose(PTD[:, ph + 64:ph + 128], WSH[0:64, c * Q:(c + 1) * Q], c_id32[0:64, 0:64])
                        t.transpose(PTD[:, ph + 128:ph + 192], DECH[0:64, c * Q:(c + 1) * Q], c_id32[0:64, 0:64])
                        pe.inc(t.transpose(PTD[:, ph + 192:ph + 256], LAH[0:64, c * Q:(c + 1) * Q],
                                           c_id32[0:64, 0:64]))
            assert ve.n == ve0 + 6 + NCH, (ve.n, ve0)
            assert se.n == se0 + 4
            assert pe.n == pe0 + NCH
            prep_tok = ve.tok()
            if dbg:
                with nc.Block() as blk:
                    @blk.gpsimd
                    def _(g):
                        wait(g, prep_tok)
                        dG.inc(g.dma_start(out=dbgt["DDT"][0:64, :], in_=DTF[:, :]), 16)
                        dG.inc(g.dma_start(out=dbgt["DLA"][0:64, :], in_=LAH[:, :]), 16)
            psx2d.close()
            ctx2d.close()

        if on("p2e"):
            ctx2e = ExitStack()
            psx2e = ExitStack()
            HZ = ctx2e.enter_context(nc.sbuf_tensor("hz", [128, D], BF16))
            XH = ctx2e.enter_context(nc.sbuf_tensor("xh", [128, 4 * 64], BF16))
            LSEGA = ctx2e.enter_context(nc.sbuf_tensor("lsega", [8, NHD], F32))
            W8 = ctx2e.enter_context(nc.sbuf_tensor("w8", [8, NHD], F32))
            WFLAT = ctx2e.enter_context(nc.sbuf_tensor("wflat", [1, 512], F32))
            WBC = ctx2e.enter_context(nc.sbuf_tensor("wbc", [128, 512], F32))
            HZS = ctx2e.enter_context(nc.sbuf_tensor("hzs", [128, D], BF16))
            PH1 = psx2e.enter_context(nc.psum_tensor("ph1", [128, 2048], F32))
            PT8 = psx2e.enter_context(nc.psum_tensor("pt8", [8, NHD], F32))
            PW = psx2e.enter_context(nc.psum_tensor("pw", [128, 512], F32))
            BAR2E = bar_toks()
            pe0, ve0 = pe.n, ve.n
            XA1 = 4
            NTOT = NCH * NHD
            xh_tok = {}
            hz_tok = {}
            _n = ve0
            for i0 in range(XA1):
                _n += 1; xh_tok[i0] = (ve, _n)
            for i0 in range(NTOT):
                if i0 + XA1 < NTOT:
                    _n += 1; xh_tok[i0 + XA1] = (ve, _n)
                _n += 1; hz_tok[i0] = (ve, _n)
            with nc.Block() as blk:
                @blk.vector
                def _(v):
                    wait(v, *BAR2E, prep_tok)
                    VC(v, v.memset(HZ[:, :], 0))

                    def em_xh(i):
                        c, r = divmod(i, NHD)
                        if i >= 4:
                            wait(v, (pe, pe0 + i - 3))
                        ins = v.tensor_scalar(XH[:, (i % 4) * 64:(i % 4) * 64 + 64],
                                              xsS(c)[:, r * 64:(r + 1) * 64],
                                              C_wscl[:, c * 64 + r:c * 64 + r + 1], None, ALU.mult)
                        assert ve.inc(ins)[1] == xh_tok[i][1]

                    for i0 in range(XA1):
                        em_xh(i0)
                    for i in range(NTOT):
                        c, r = divmod(i, NHD)
                        if i + XA1 < NTOT:
                            em_xh(i + XA1)
                        wait(v, (pe, pe0 + i + 1))
                        if c >= 1:
                            wait(v, hz_tok[i - NHD])
                        ins = v.scalar_tensor_tensor(HZ[:, r * 64:(r + 1) * 64],
                                                     HZ[:, r * 64:(r + 1) * 64],
                                                     C_dec[:, c * 64 + r:c * 64 + r + 1],
                                                     PH1[:, (i % 4) * 512:(i % 4) * 512 + 64],
                                                     ALU.mult, ALU.add)
                        assert ve.inc(ins)[1] == hz_tok[i][1]

                @blk.tensor
                def _(t):
                    wait(t, *BAR2E)
                    for c in range(NCH):
                        for r in range(NHD):
                            i = c * NHD + r
                            wait(t, xh_tok[i])
                            if i >= 4:
                                wait(t, hz_tok[i - 4])
                            pe.inc(t.matmul(PH1[:, (i % 4) * 512:(i % 4) * 512 + 64],
                                            BS16[:, c * 128:(c + 1) * 128],
                                            XH[:, (i % 4) * 64:(i % 4) * 64 + 64],
                                            start=True, stop=True))
            assert pe.n == pe0 + NCH * NHD
            assert ve.n == _n
            hz_done = hz_tok[NCH * NHD - 1]

            ve1 = ve.n
            with nc.Block() as blk:
                @blk.gpsimd
                def _(g):
                    wait(g, hz_done, prep_tok)
                    dCC.inc(g.dma_start(out=cc2hi.ap(), in_=HZ[:, :]), 16)
                    dCC.inc(g.dma_start(out=cc2li.ap(), in_=LSEG[:, :]), 16)
                    wait(g, dCC.tok())
                    ccs.inc(g.collective_compute(
                        "AllGather", ALU.bypass, replica_groups=[list(range(NCORE))],
                        ins=[cc2hi.ap()], outs=[cc2ho.ap()]), 1)
                    ccs.inc(g.collective_compute(
                        "AllGather", ALU.bypass, replica_groups=[list(range(NCORE))],
                        ins=[cc2li.ap()], outs=[cc2lo.ap()]), 1)
                    wait(g, ccs.tok())
                    dCC.inc(g.dma_start(out=LSEGA[:, :], in_=cc2lo.ap()), 16)
            lsega_tok = dCC.tok()

            pe1, se1 = pe.n, se.n
            with nc.Block() as blk:
                @blk.tensor
                def _(t):
                    wait(t, lsega_tok)
                    pe.inc(t.matmul(PT8[:, :], c_selr2[:, :], LSEGA[:, :], start=True, stop=True))

                @blk.scalar
                def _(s):
                    wait(s, (pe, pe1 + 1))
                    se.inc(s.activation(W8[:, :], PT8[:, :], AF.Exp))

                @blk.vector
                def _(v):
                    wait(v, (se, se1 + 1))
                    ve.inc(v.tensor_tensor(W8[:, :], W8[:, :], c_pfx8[:, :], ALU.mult))
            w8_tok = ve.tok()

            ve2, pe2 = ve.n, pe.n
            hzs_tok = {}
            wflat_tok = [None]
            with nc.Block() as blk:
                @blk.gpsimd
                def _(g):
                    wait(g, w8_tok)
                    wflat_tok[0] = dCC.inc(g.dma_start(out=WFLAT[0:1, :], in_=W8[:, :]), 16)
                    for j in range(NCORE):
                        if j >= 1:
                            wait(g, (ve, ve2 + 1 + j * NHD))
                        hzs_tok[j] = dX[j % 2].inc(
                            g.dma_start(out=HZS[:, :],
                                        in_=cc2ho.ap()[j * 128:(j + 1) * 128, :]), 16)

                @blk.tensor
                def _(t):
                    wait(t, wflat_tok[0])
                    last = None
                    for j in range(NCORE):
                        last = t.matmul(PW[:, j * 64:(j + 1) * 64], c_or32[0:1, :],
                                        WFLAT[0:1, j * 64:(j + 1) * 64], start=True, stop=True)
                    pe.inc(last)

                @blk.vector
                def _(v):
                    wait(v, (pe, pe2 + 1))
                    inc_chain(v, ve, v.tensor_copy(WBC[:, :], PW[:, :]))
                    for j in range(NCORE):
                        wait(v, hzs_tok[j])
                        if j >= 1:
                            wait(v, (ve, ve2 + 1 + j * NHD))
                        for r in range(NHD):
                            dst = HRUN[:, r * 64:(r + 1) * 64]
                            src = HZS[:, r * 64:(r + 1) * 64]
                            wcol = WBC[:, j * 64 + r:j * 64 + r + 1]
                            if j == 0:
                                ins = v.tensor_scalar(dst, src, wcol, None, ALU.mult)
                            else:
                                ins = v.scalar_tensor_tensor(dst, src, wcol, dst, ALU.mult, ALU.add)
                            ve.inc(ins)
            assert ve.n == ve2 + 1 + NCORE * NHD
            hin_tok = ve.tok()
            if dbg:
                with nc.Block() as blk:
                    @blk.gpsimd
                    def _(g):
                        wait(g, hz_done)
                        dG.inc(g.dma_start(out=dbgt["DHZ"], in_=HZ[:, :]), 16)
                        wait(g, hin_tok)
                        dG.inc(g.dma_start(out=dbgt["DHI"], in_=HRUN[:, :]), 16)
            psx2e.close()
            ctx2e.close()

        if on("p2f"):
            ctx2f = ExitStack()
            psx2f = ExitStack()
            GM16 = ctx2f.enter_context(nc.sbuf_tensor("gm16", [128, 128], BF16))
            GM2 = ctx2f.enter_context(nc.sbuf_tensor("gm2", [128, 256], BF16))
            CT2 = ctx2f.enter_context(nc.sbuf_tensor("ct2", [128, 256], BF16))
            LACB = ctx2f.enter_context(nc.sbuf_tensor("lacb", [128, 4 * 128], F32))
            DM = ctx2f.enter_context(nc.sbuf_tensor("dm", [128, 4 * 128], F32))
            E16 = BIGB[:, NT * SL + 1024:NT * SL + 1536]
            EL16 = BIGB[:, NT * SL + 1536:NT * SL + 2048]
            MT16 = ctx2f.enter_context(nc.sbuf_tensor("mt16", [128, 4 * 128], BF16))
            CH16 = ctx2f.enter_context(nc.sbuf_tensor("ch16", [128, 4 * 128], BF16))
            XH2 = ctx2f.enter_context(nc.sbuf_tensor("xh2", [128, 4 * 64], BF16))
            G16 = ctx2f.enter_context(nc.sbuf_tensor("g16", [128, NT * 128], BF16))
            GSQ = BIGB[:, NT * SL:NT * SL + 1024]
            SZC = ctx2f.enter_context(nc.sbuf_tensor("szc", [128, NT * 128], BF16))
            RMSR = ctx2f.enter_context(nc.sbuf_tensor("rmsr", [1, 128], F32))
            PGX = psx2f.enter_context(nc.psum_tensor("pgx", [128, 512], F32))
            PD2 = psx2f.enter_context(nc.psum_tensor("pd2", [128, 3 * 512], F32))
            PY = psx2f.enter_context(nc.psum_tensor("py", [128, 1024], F32))
            PH2 = psx2f.enter_context(nc.psum_tensor("ph2", [128, 1024], F32))
            BAR2F = bar_toks()
            pe0, ve0, se0 = pe.n, ve.n, se.n
            PG = PGX[:, 0:128]
            PRB = PGX[:, 128:256]
            PSS = PGX[0:1, 256:384]

            PE_PC = 1 + NHD * 3 + 5
            SE_PC = 2 * NHD + 5
            VEH = 7
            VPAIR = 2 * VEH + 1
            VE_PC = 1 + (NHD // 2) * VPAIR + NT

            def pe_gcb(c):
                return (pe, pe0 + c * PE_PC + 1)

            def pe_pdtr(c, r):
                return (pe, pe0 + c * PE_PC + 1 + 3 * r + 1)

            def pe_y(c, r):
                return (pe, pe0 + c * PE_PC + 1 + 3 * r + 2)

            def pe_hc(c, r):
                return (pe, pe0 + c * PE_PC + 1 + 3 * r + 3)

            def pe_ssk(c, k):
                return (pe, pe0 + c * PE_PC + 1 + 3 * NHD + 1 + k)

            def pe_ss(c):
                return (pe, pe0 + c * PE_PC + 1 + 3 * NHD + 4)

            def pe_rmsbc(c):
                return (pe, pe0 + c * PE_PC + PE_PC)

            def se_edm(c, r):
                return (se, se0 + c * SE_PC + 2 * r + 1)

            def se_ela(c, r):
                return (se, se0 + c * SE_PC + 2 * r + 2)

            def se_gsqk(c, k):
                return (se, se0 + c * SE_PC + 2 * NHD + 1 + k)

            def se_rms(c):
                return (se, se0 + c * SE_PC + SE_PC)

            def vh(c, r, o):
                return (ve, ve0 + c * VE_PC + 1 + (r // 2) * VPAIR + (r % 2) * VEH + o)

            def ve_gm(c):
                return (ve, ve0 + c * VE_PC + 1)

            def ve_lacb(c, r):
                return vh(c, r, 1)

            def ve_dm(c, r):
                return vh(c, r, 2)

            def ve_mta(c, r):
                return vh(c, r, 3)

            def ve_mtb(c, r):
                return vh(c, r, 4)

            def ve_ch(c, r):
                return vh(c, r, 5)

            def ve_xh(c, r):
                return vh(c, r, 6)

            def ve_hup(c, r):
                return vh(c, r, 7)

            def ve_gev(c, q):
                return (ve, ve0 + c * VE_PC + 1 + (q + 1) * VPAIR)

            def ve_yn(c):
                return (ve, ve0 + (c + 1) * VE_PC)

            LA_A, DM_A, MT_A = 6, 4, 1
            PDTR_A = 5

            def q_pair(r):
                return r // 2

            def plan_vector():
                n = ve0
                toks = {}
                for c in range(NCH):
                    n += 1
                    toks[("gm", c)] = n
                    n += 2
                    toks[("gm2", c)] = n
                    for r0 in range(LA_A):
                        n += 1
                        toks[("lacb", c, r0)] = n
                    for r0 in range(DM_A):
                        n += 1
                        toks[("dm", c, r0)] = n
                    n += 1; toks[("mta", c, 0)] = n
                    n += 1; toks[("mtb", c, 0)] = n
                    n += 1; toks[("mtb", c, 1)] = n
                    n += 1; toks[("ch", c, 0)] = n
                    n += 1; toks[("xh", c, 0)] = n
                    n += 1; toks[("xh", c, 1)] = n
                    for r in range(NHD):
                        if r + LA_A < NHD:
                            n += 1; toks[("lacb", c, r + LA_A)] = n
                        if r + DM_A < NHD:
                            n += 1; toks[("dm", c, r + DM_A)] = n
                        if r % 2 == 1 and (r + 1) // 2 < NHD // 2:
                            qn = (r + 1) // 2
                            n += 1; toks[("mta", c, qn)] = n
                            n += 1; toks[("mtb", c, 2 * qn)] = n
                            n += 1; toks[("mtb", c, 2 * qn + 1)] = n
                            n += 1; toks[("ch", c, qn)] = n
                            n += 1; toks[("xh", c, 2 * qn)] = n
                            n += 1; toks[("xh", c, 2 * qn + 1)] = n
                        n += 1; toks[("hup", c, r)] = n
                        if r % 2 == 1:
                            n += 1; toks[("gev", c, r // 2)] = n
                    for k in range(NT):
                        n += 1
                    toks[("yn", c)] = n
                return toks, n

            def plan_tensor():
                n = pe0
                toks = {}
                for c in range(NCH):
                    n += 1; toks[("gcb", c)] = n
                    for r0 in range(PDTR_A):
                        n += 1; toks[("pdtr", c, r0)] = n
                    for r in range(NHD):
                        if r + PDTR_A < NHD:
                            n += 1; toks[("pdtr", c, r + PDTR_A)] = n
                        n += 1; toks[("y", c, r)] = n
                        n += 1; toks[("hc", c, r)] = n
                    for qq in range(4):
                        n += 1; toks[("ssk", c, qq)] = n
                    n += 1; toks[("rmsbc", c)] = n
                return toks, n

            VTOK, ve_end = plan_vector()
            TTOK, pe_end = plan_tensor()

            def vt(*k):
                return (ve, VTOK[k])

            def tt(*k):
                return (pe, TTOK[k])

            def se_edm(c, r):
                return (se, se0 + c * SE_PC + 2 * r + 1)

            def se_ela(c, r):
                return (se, se0 + c * SE_PC + 2 * r + 2)

            def se_gsqk(c, k):
                return (se, se0 + c * SE_PC + 2 * NHD + 1 + k)

            def se_rms(c):
                return (se, se0 + c * SE_PC + SE_PC)

            szc_tok = {}
            with nc.Block() as blk:
                @blk.gpsimd
                def _(g):
                    wait(g, *BAR2F)
                    for c in range(NCH):
                        if c >= 1:
                            wait(g, (ve, VTOK[("gev", c - 1, NHD // 2 - 1)]))
                        wait(g, zst_tok[NT - 2], zst_tok[NT - 1])
                        szc_tok[c] = dR[c % 2].inc(g.dma_start(
                            out=SZC[:, :].rearrange("p (k t) -> p k t", k=NT),
                            in_=ZSP.rearrange("(k p) t -> p k t", p=128)[:, :, c * 128:(c + 1) * 128]), 16)

                @blk.vector
                def _(v):
                    wait(v, *BAR2F)

                    def em_lacb(c, r):
                        i = c * NHD + r
                        if i >= 4:
                            pr = i - 4
                            wait(v, tt("pdtr", pr // NHD, pr % NHD))
                        ins = v.tensor_scalar(LACB[:, (i % 4) * 128:(i % 4 + 1) * 128],
                                              c_ones32[:, :],
                                              C_laS[:, c * 64 + r:c * 64 + r + 1], None, ALU.mult)
                        assert ve.inc(ins)[1] == VTOK[("lacb", c, r)]

                    def em_dm(c, r):
                        i = c * NHD + r
                        wait(v, tt("pdtr", c, r))
                        if i >= 4:
                            pr = i - 4
                            wait(v, se_edm(pr // NHD, pr % NHD))
                        ins = v.tensor_scalar(DM[:, (i % 4) * 128:(i % 4 + 1) * 128],
                                              PD2[:, (i % 3) * 512:(i % 3) * 512 + 128],
                                              C_bias[:, c * 64 + r:c * 64 + r + 1],
                                              60.0, ALU.add, ALU.min)
                        assert ve.inc(ins)[1] == VTOK[("dm", c, r)]

                    def em_pair(c, qn):
                        r0 = 2 * qn
                        i = c * NHD + r0
                        i4 = i % 4
                        wait(v, se_edm(c, r0 + 1))
                        if i >= 4:
                            pr = i - 4
                            wait(v, tt("y", pr // NHD, pr % NHD + 1))
                        ins = v.tensor_tensor(MT16[:, i4 * 128:(i4 + 2) * 128],
                                              E16[:, i4 * 128:(i4 + 2) * 128],
                                              GM2[:, :], ALU.mult)
                        assert ve.inc(ins)[1] == VTOK[("mta", c, qn)]
                        wait(v, (ve, VTOK[("mta", c, qn)]))
                        for rr in (r0, r0 + 1):
                            j4 = (c * NHD + rr) % 4
                            ins = v.scalar_tensor_tensor(MT16[:, j4 * 128:(j4 + 1) * 128],
                                                         c_id16[:, :], c_drep[:, rr:rr + 1],
                                                         MT16[:, j4 * 128:(j4 + 1) * 128],
                                                         ALU.mult, ALU.add)
                            assert ve.inc(ins)[1] == VTOK[("mtb", c, rr)]
                        wait(v, se_ela(c, r0 + 1))
                        ins = v.tensor_tensor(CH16[:, i4 * 128:(i4 + 2) * 128],
                                              CT2[:, :],
                                              EL16[:, i4 * 128:(i4 + 2) * 128], ALU.mult)
                        assert ve.inc(ins)[1] == VTOK[("ch", c, qn)]
                        if i >= 4:
                            pr = i - 4
                            wait(v, tt("hc", pr // NHD, pr % NHD + 1))
                        for rr in (r0, r0 + 1):
                            j4 = (c * NHD + rr) % 4
                            ins = v.tensor_scalar(XH2[:, j4 * 64:(j4 + 1) * 64],
                                                  xsS(c)[:, rr * 64:(rr + 1) * 64],
                                                  C_wscl[:, c * 64 + rr:c * 64 + rr + 1], None, ALU.mult)
                            assert ve.inc(ins)[1] == VTOK[("xh", c, rr)]

                    for c in range(NCH):
                        wait(v, tt("gcb", c))
                        if c >= 1:
                            wait(v, (ve, VTOK[("yn", c - 1)]))
                        ins = v.tensor_tensor(GM16[:, :], PG, c_mask01[:, :], ALU.mult)
                        assert ve.inc(ins)[1] == VTOK[("gm", c)]
                        wait(v, (ve, VTOK[("gm", c)]))
                        v.tensor_copy(GM2[:, 0:128], GM16[:, :])
                        ve.inc(v.tensor_copy(GM2[:, 128:256], GM16[:, :]))
                        VC(v, v.tensor_copy(CT2[:, 0:128], CT16[:, c * 128:(c + 1) * 128]))
                        ins = v.tensor_copy(CT2[:, 128:256], CT16[:, c * 128:(c + 1) * 128])
                        assert ve.inc(ins)[1] == VTOK[("gm2", c)]
                        wait(v, (ve, VTOK[("gm2", c)]))
                        for r0 in range(LA_A):
                            em_lacb(c, r0)
                        for r0 in range(DM_A):
                            em_dm(c, r0)
                        em_pair(c, 0)
                        for r in range(NHD):
                            if r + LA_A < NHD:
                                em_lacb(c, r + LA_A)
                            if r + DM_A < NHD:
                                em_dm(c, r + DM_A)
                            if r % 2 == 1 and (r + 1) // 2 < NHD // 2:
                                em_pair(c, (r + 1) // 2)
                            i = c * NHD + r
                            if c == 0 and r == 0:
                                wait(v, hin_tok)
                            wait(v, tt("hc", c, r))
                            if c >= 1:
                                wait(v, (ve, VTOK[("hup", c - 1, r)]))
                            ins = v.scalar_tensor_tensor(HRUN[:, r * 64:(r + 1) * 64],
                                                         HRUN[:, r * 64:(r + 1) * 64],
                                                         C_dec[:, c * 64 + r:c * 64 + r + 1],
                                                         PH2[:, (i % 2) * 512:(i % 2) * 512 + 64],
                                                         ALU.mult, ALU.add)
                            assert ve.inc(ins)[1] == VTOK[("hup", c, r)]
                            if r % 2 == 1:
                                q = r // 2
                                wait(v, tt("y", c, r), szc_tok[c])
                                ins = v.tensor_tensor(G16[:, q * 128:(q + 1) * 128],
                                                      PY[:, ((i // 2) % 2) * 512:((i // 2) % 2) * 512 + 128],
                                                      SZC[:, q * 128:(q + 1) * 128], ALU.mult)
                                assert ve.inc(ins)[1] == VTOK[("gev", c, q)]
                        wait(v, tt("rmsbc", c))
                        last_v = None
                        for k in range(NT):
                            last_v = ve.inc(v.scalar_tensor_tensor(
                                BIGB[:, k * SL + c * 128:k * SL + (c + 1) * 128],
                                G16[:, k * 128:(k + 1) * 128],
                                c_nrmw[:, k:k + 1],
                                PRB, ALU.mult, ALU.mult))[1]
                        assert last_v == VTOK[("yn", c)]

                @blk.scalar
                def _(s):
                    wait(s, *BAR2F)
                    for c in range(NCH):
                        for r in range(NHD):
                            i = c * NHD + r
                            i4 = i % 4
                            i3 = i % 3
                            wait(s, (se, se0 + c * SE_PC + 2 * r))
                            wait(s, (ve, VTOK[("dm", c, r)]))
                            if i >= 4:
                                pr = i - 4
                                wait(s, (ve, VTOK[("mta", pr // NHD, (pr % NHD) // 2)]))
                            se.inc(s.activation(E16[:, i4 * 128:(i4 + 1) * 128],
                                                DM[:, i4 * 128:(i4 + 1) * 128], AF.Exp))
                            wait(s, tt("pdtr", c, r))
                            if i >= 4:
                                pr = i - 4
                                wait(s, (ve, VTOK[("ch", pr // NHD, (pr % NHD) // 2)]))
                            se.inc(s.activation(EL16[:, i4 * 128:(i4 + 1) * 128],
                                                PD2[:, i3 * 512:i3 * 512 + 128], AF.Exp))
                        QW = NT * 32
                        wait(s, (ve, VTOK[("gev", c, NHD // 2 - 1)]))
                        for qq in range(4):
                            if qq >= 1:
                                wait(s, tt("ssk", c, qq - 1))
                            se.inc(s.activation(GSQ[:, :], G16[:, qq * QW:(qq + 1) * QW], AF.Square))
                        wait(s, tt("ssk", c, 3))
                        SC(s, s.activation(RMSR[0:1, :], PSS, AF.Ln,
                                           bias=c_eps1[0:1, :], scale=1.0 / D))
                        se.inc(s.activation(RMSR[0:1, :], RMSR[0:1, :], AF.Exp, scale=-0.5))

                @blk.tensor
                def _(t):
                    wait(t, *BAR2F)

                    def em_pdtr(c, r):
                        i = c * NHD + r
                        wait(t, vt("lacb", c, r))
                        if i >= 3:
                            pr = i - 3
                            wait(t, se_ela(pr // NHD, pr % NHD), vt("dm", pr // NHD, pr % NHD))
                        ins = t.transpose(PD2[:, (i % 3) * 512:(i % 3) * 512 + 128],
                                          LACB[:, (i % 4) * 128:(i % 4 + 1) * 128], c_id32[:, :])
                        assert pe.inc(ins)[1] == TTOK[("pdtr", c, r)]

                    for c in range(NCH):
                        if c >= 1:
                            wait(t, vt("yn", c - 1))
                        ins = t.matmul(PG, BT16[:, c * 128:(c + 1) * 128],
                                       CT16[:, c * 128:(c + 1) * 128], start=True, stop=True)
                        assert pe.inc(ins)[1] == TTOK[("gcb", c)]
                        for r0 in range(PDTR_A):
                            em_pdtr(c, r0)
                        for r in range(NHD):
                            i = c * NHD + r
                            i4 = i % 4
                            if r + PDTR_A < NHD:
                                em_pdtr(c, r + PDTR_A)
                            if c == 0 and r == 0:
                                wait(t, hin_tok)
                            wait(t, vt("mtb", c, r), vt("ch", c, q_pair(r)))
                            if (i // 2) >= 2 and r % 2 == 0:
                                pq = (i // 2) - 2
                                wait(t, vt("gev", pq // (NHD // 2), pq % (NHD // 2)))
                            pb = (r % 2) * 64
                            tp = (0, pb)
                            ysl = ((i // 2) % 2) * 512
                            t.matmul(PY[pb:pb + 64, ysl:ysl + 128],
                                     xsS(c)[:, r * 64:(r + 1) * 64],
                                     MT16[:, i4 * 128:(i4 + 1) * 128],
                                     start=True, stop=False, tile_position=tp)
                            ins = t.matmul(PY[pb:pb + 64, ysl:ysl + 128],
                                           HRUN[:, r * 64:(r + 1) * 64],
                                           CH16[:, i4 * 128:(i4 + 1) * 128],
                                           start=False, stop=True, tile_position=tp)
                            assert pe.inc(ins)[1] == TTOK[("y", c, r)]
                            wait(t, vt("xh", c, r))
                            if i >= 2:
                                pr = i - 2
                                wait(t, vt("hup", pr // NHD, pr % NHD))
                            ins = t.matmul(PH2[:, (i % 2) * 512:(i % 2) * 512 + 64],
                                           BS16[:, c * 128:(c + 1) * 128],
                                           XH2[:, i4 * 64:(i4 + 1) * 64],
                                           start=True, stop=True)
                            assert pe.inc(ins)[1] == TTOK[("hc", c, r)]
                        last = None
                        for qq in range(4):
                            wait(t, se_gsqk(c, qq))
                            for k in range(NT // 4):
                                last = t.matmul(PSS, c_oc16[:, :], GSQ[:, k * 128:(k + 1) * 128],
                                                start=(qq == 0 and k == 0),
                                                stop=(qq == 3 and k == NT // 4 - 1))
                            assert pe.inc(last)[1] == TTOK[("ssk", c, qq)]
                        wait(t, se_rms(c), vt("gm", c))
                        ins = t.matmul(PRB, c_or32[0:1, :], RMSR[0:1, :], start=True, stop=True)
                        assert pe.inc(ins)[1] == TTOK[("rmsbc", c)]
            assert pe.n == pe_end
            assert se.n == se0 + NCH * SE_PC
            assert ve.n == ve_end
            yn_tok = ve.tok()
            if dbg:
                with nc.Block() as blk:
                    @blk.gpsimd
                    def _(g):
                        wait(g, yn_tok)
                        dG.inc(g.dma_start(out=dbgt["DYN"], in_=BIGB[:, 0:NT * SL]), 16)
            psx2f.close()
            ctx2f.close()

        if on("p2d"):
            ctxSSD.close()
        if on("p2c"):
            ctxBCB.close()
        if on("p2b"):
            ctxDTR.close()

        if on("p2h"):
            with nc.Block() as blk:
                ctxh, psxh, stats3_raw, x2st_tok_by_tile = ffn_pass(
                    blk, lambda k: BIGB[:, k * SL:(k + 1) * SL], yn_tok,
                    lambda m: strip_view(wop, m), "cast",
                    XR, XR, True, 1.0, False)
            psxh.close()
            ctxh.close()

        if on("p3"):
            with nc.Block() as blk:
                rows3_tok = emit_ln_rows(blk, False, [stats3_raw])

            ctx3 = ExitStack()
            psx3 = ExitStack()
            x2t = ctx3.enter_context(nc.sbuf_tensor("x2t", [128, 2 * SL], F32))
            tmp3 = ctx3.enter_context(nc.sbuf_tensor("tmp3", [128, SL], F32))
            SCB3 = ctx3.enter_context(nc.sbuf_tensor("scb3", [128, SL], F32))
            MSB3 = ctx3.enter_context(nc.sbuf_tensor("msb3", [128, SL], F32))
            Pba3 = psx3.enter_context(nc.psum_tensor("Pba3", [128, SL], F32))
            Pbb3 = psx3.enter_context(nc.psum_tensor("Pbb3", [128, SL], F32))
            pe0, ve0 = pe.n, ve.n
            BAR3 = bar_toks()
            x2t_tok = {}
            with nc.Block() as blk:
                @blk.sync
                def _(sy):
                    wait(sy, *BAR3)
                    for k in range(NT):
                        if k >= 2:
                            wait(sy, (ve, ve0 + 3 * (k - 2) + 3))
                        wait(sy, x2st_tok_by_tile[k])
                        x2t_tok[k] = dRH[k % 2].inc(
                            sy.dma_start(out=x2t[:, (k % 2) * SL:(k % 2 + 1) * SL],
                                         in_=XR[k * 128:(k + 1) * 128, :]), 16)

                bc3 = emit_bcast_rows(blk, rows3_tok, Pba3, Pbb3, r_scl, r_mscl,
                                      extra=BAR3)

                @blk.vector
                def _(v):
                    wait(v, *BAR3, bc3)
                    v.tensor_copy(SCB3[:, :], Pba3[:, :])
                    VC(v, v.tensor_copy(MSB3[:, :], Pbb3[:, :]))
                    for k in range(NT):
                        wait(v, x2t_tok[k])
                        sl = (k % 2) * SL
                        inc_chain(v, ve, v.tensor_tensor(tmp3[:, :], x2t[:, sl:sl + SL],
                                                         SCB3[:, :], ALU.mult))
                        inc_chain(v, ve, v.tensor_tensor(tmp3[:, :], tmp3[:, :],
                                                         MSB3[:, :], ALU.subtract))
                        inc_chain(v, ve, v.tensor_copy(biga_t(k), tmp3[:, :]))
            ln3_tok = ve.tok()
            psx3.close()
            ctx3.close()

            with nc.Block() as blk:
                ctxF, psxF, _, _ = ffn_pass(
                    blk, biga_t, ln3_tok,
                    sgn_strip_view, "sgn16",
                    XR, out_d, False, 0.5, True)
            psxF.close()
            ctxF.close()

        with nc.Block() as blk:
            @blk.sync
            def _(sy):
                wait(sy, *bar_toks())
                if limit_i < PHASES.index("p3"):
                    for m in range(NT):
                        fin.inc(sy.dma_start(out=out_d[m * 128:(m + 1) * 128, :],
                                             in_=XR[m * 128:(m + 1) * 128, :]), 16)
                wait(sy, fin.tok())
    return nc


def make_in_maps(inputs):
    import ml_dtypes
    _ = np.asarray(inputs["L0_attn_residual"]).shape
    x = np.ascontiguousarray(np.asarray(inputs["L0_attn_hidden_states"], dtype=np.float32)[0])
    Wf = np.ascontiguousarray(np.asarray(inputs["L0_ffn_weight"], dtype=np.float32))
    bf = np.asarray(inputs["L0_ffn_bias"], dtype=np.float32)
    lnw = np.asarray(inputs["L0_attn_ln_w"], dtype=np.float32)
    Wip = np.ascontiguousarray(np.asarray(inputs["L0_attn_in_proj_w"], dtype=np.float32))
    cw = np.asarray(inputs["L0_attn_conv_w"], dtype=np.float32)
    alog = np.asarray(inputs["L0_attn_A_log"], dtype=np.float32)
    dtb = np.asarray(inputs["L0_attn_dt_bias"], dtype=np.float32)
    Dv = np.asarray(inputs["L0_attn_D"], dtype=np.float32)
    nrmw = np.asarray(inputs["L0_attn_norm_w"], dtype=np.float32)
    Wop = np.ascontiguousarray(np.asarray(inputs["L0_attn_out_proj_w"], dtype=np.float32))
    mup = np.asarray(inputs["L0_attn_mup_vector"], dtype=np.float32)

    xT = np.ascontiguousarray(x.T)
    wffn_t = np.ascontiguousarray(Wf.T)
    wip_pad = np.zeros((D, PROJ_PAD), np.float32)
    wip_pad[:, :8512] = Wip
    mup_pad = np.zeros(PROJ_PAD, np.float32)
    mup_pad[:8512] = mup

    cols = lambda vec, nt: np.ascontiguousarray(vec.reshape(nt, 128).T)
    dtb_c = np.zeros((128, 1), np.float32); dtb_c[:64, 0] = dtb
    alog_c = np.zeros((128, 1), np.float32); alog_c[:64, 0] = alog
    s_idx = np.arange(128)[:, None]
    t_idx = np.arange(128)[None, :]

    shared = dict(
        wffn=wffn_t, wip=wip_pad, wop=Wop,
        lnw_c=cols(lnw, NT), lnw_r=np.ascontiguousarray(lnw.reshape(1, D)),
        nrmw_c=cols(nrmw, NT), bias_c=cols(bf, NT),
        mup_c=np.ascontiguousarray(mup_pad.reshape(MT_IP, 128).T),
        cw_c=np.ascontiguousarray(cw.reshape(CONVT, 128, 4).transpose(1, 0, 2).reshape(128, CONVT * 4)),
        drep=np.ascontiguousarray(np.tile(Dv[None, :], (128, 1))),
        dtb=dtb_c, alog=alog_c,
        or32=np.ones((1, 128), np.float32),
        id32=np.eye(128, dtype=np.float32),
        mask01=(s_idx <= t_idx).astype(np.float32),
        oc16=np.ones((128, 1), dtype=ml_dtypes.bfloat16),
        id16=np.eye(128).astype(ml_dtypes.bfloat16),
        ones32=np.ones((128, 128), np.float32),
    )

    in_maps = []
    for c in range(NCORE):
        hsel = np.zeros((128, 8), np.float32)
        if c > 0:
            hsel[:, c - 1] = 1.0
        pfx8 = np.zeros((8, NHD), np.float32)
        pfx8[:c, :] = 1.0
        selr2 = np.zeros((8, 8), np.float32)
        for j in range(8):
            for k in range(8):
                if j < k < c:
                    selr2[k, j] = 1.0
        m = dict(shared)
        m["xin"] = np.ascontiguousarray(xT[:, c * SL:(c + 1) * SL])
        m["hsel"] = hsel
        m["pfx8"] = pfx8
        m["selr2"] = selr2
        in_maps.append(m)
    return in_maps


def kernel(**inputs):
    nc = build_program()
    in_maps = make_in_maps(inputs)
    res = run_bass_kernel_spmd(nc, in_maps, core_ids=list(range(NCORE)))
    out = np.concatenate([np.asarray(r["out"]).T for r in res.results], axis=0)[None]
    return np.ascontiguousarray(out.astype(np.float32))
